# version 16 (final)
# speedup vs baseline: 1.1725x; 1.0314x over previous
import numpy as np
import ml_dtypes
import concourse.bass as bass
STOP_AFTER = 99
import concourse.mybir as mybir
from concourse.bass_utils import run_bass_kernel_spmd

F32 = mybir.dt.float32
BF16 = mybir.dt.bfloat16
AF = mybir.ActivationFunctionType
ALU = mybir.AluOpType
BF = ml_dtypes.bfloat16

T_TILES = 16
C_CHUNKS = 8
TOK = 2048
DM = 1024
LN_EPS = 1e-5


class Sem:
    def __init__(self, handle, step=1):
        self.h = handle
        self.step = step
        self.n = 0

    def bump(self, k=1):
        self.n += k * self.step
        return self.n


class Prog:
    def __init__(self, nc, bufs, sems, dram):
        self.nc = nc
        self.b = bufs
        self.S = sems
        self.dram = dram
        self.ops = {k: [] for k in ("sync", "gpsimd", "tensor", "vector",
                                    "scalar")}
        self.prep_guards = None
        self.pending_stats = None
        self.want_tail_stats = False

    def emit(self, engine, fn):
        self.ops[engine].append(fn)


def wslot(b, slot):
    return b["wbuf"][:, slot, :].rearrange("p (c e) -> p c e", c=C_CHUNKS)


def aslot(b_, s):
    a3 = b_["arena"][:].rearrange("p (c t) -> p c t", c=C_CHUNKS)
    return a3[:, :, (14 + s) * 128:(15 + s) * 128]


def emit_stats(p, prev_v, prev_g):
    b, S = p.b, p.S
    pend = p.pending_stats
    p.pending_stats = None
    if pend is None:
        v_sync_pts = [S["v"].bump() for _ in range(7)]
        def stats(v, pw=prev_v, pgg=prev_g):
            if pw:
                v.wait_ge(S["v"].h, pw)
            if pgg:
                v.wait_ge(S["g"].h, pgg)
            for T in range(T_TILES):
                v.bn_stats(b["bn6a"][:, T, 0:6], b["x"][:, T, 0:512])
                last = v.bn_stats(b["bn6a"][:, T, 6:12],
                                  b["x"][:, T, 512:1024])
            last.then_inc(S["v"].h, 1); v.wait_ge(S["v"].h, v_sync_pts[0])
            stats_combine(v, b, S, v_sync_pts[1:6])
        p.emit("vector", stats)
        v_stats = v_sync_pts[6]
    else:
        v_sync_pts = [S["v"].bump() for _ in range(6)]
        def stats(v, pw=pend[-1]):
            v.wait_ge(S["v"].h, pw)
            stats_combine(v, b, S, v_sync_pts[0:5])
        p.emit("vector", stats)
        v_stats = v_sync_pts[5]

    s_sqrt = S["s"].bump()
    def sqrt_phase(s, w=v_stats):
        s.wait_ge(S["v"].h, w)
        s.activation(b["sqv"][:], b["varpe"][:], AF.Sqrt).then_inc(S["s"].h, 1)
    p.emit("scalar", sqrt_phase)

    v_rstd = S["v"].bump()
    def rstd_phase(v, w=s_sqrt):
        v.wait_ge(S["s"].h, w)
        v.reciprocal(b["rstd"][:], b["sqv"][:]
                     ).then_inc(S["v"].h, 1); v.wait_ge(S["v"].h, v_rstd)
    p.emit("vector", rstd_phase)




def stats_combine(v, b, S, sync_pts):
    ALU_ = ALU
    m = [b["bn6a"][:, :, k] for k in (1, 4, 7, 10)]
    cv = [b["bn6a"][:, :, k] for k in (2, 5, 8, 11)]
    W = b["stw"]
    v.tensor_add(W[:, :, 0], m[0], m[1])
    v.tensor_add(W[:, :, 1], m[2], m[3])
    v.tensor_mul(W[:, :, 2], m[0], m[0])
    v.tensor_mul(W[:, :, 3], m[1], m[1])
    v.tensor_mul(W[:, :, 4], m[2], m[2])
    v.tensor_mul(W[:, :, 5], m[3], m[3])
    v.tensor_add(W[:, :, 6], cv[0], cv[1])
    v.tensor_add(W[:, :, 7], cv[2], cv[3]
                 ).then_inc(S["v"].h, 1); v.wait_ge(S["v"].h, sync_pts[0])
    v.tensor_add(b["m_s"][:], W[:, :, 0], W[:, :, 1])
    v.tensor_add(W[:, :, 2], W[:, :, 2], W[:, :, 3])
    v.tensor_add(W[:, :, 4], W[:, :, 4], W[:, :, 5])
    v.tensor_add(W[:, :, 6], W[:, :, 6], W[:, :, 7]
                 ).then_inc(S["v"].h, 1); v.wait_ge(S["v"].h, sync_pts[1])
    v.tensor_scalar(b["negmean"][:], b["m_s"][:], -0.25, None, ALU_.mult)
    v.tensor_add(b["dm2"][:], W[:, :, 2], W[:, :, 4])
    v.tensor_scalar(b["cv_s"][:], W[:, :, 6], 1.0 / 1024.0, LN_EPS,
                    ALU_.mult, ALU_.add
                    ).then_inc(S["v"].h, 1); v.wait_ge(S["v"].h, sync_pts[2])
    v.tensor_mul(b["dm"][:], b["negmean"][:], b["negmean"][:])
    v.tensor_scalar(b["dm2"][:], b["dm2"][:], 0.25, None, ALU_.mult
                    ).then_inc(S["v"].h, 1); v.wait_ge(S["v"].h, sync_pts[3])
    v.tensor_sub(b["dm2"][:], b["dm2"][:], b["dm"][:]
                 ).then_inc(S["v"].h, 1); v.wait_ge(S["v"].h, sync_pts[4])
    v.tensor_add(b["varpe"][:], b["dm2"][:], b["cv_s"][:]
                 ).then_inc(S["v"].h, 1)




def stats_combine16(v, b, S, sync_pts):
    mview = b["bn6b"][:].rearrange("p t g s -> p t (g s)")[
        :, :, 1::3]
    cvview = b["bn6b"][:].rearrange("p t g s -> p t (g s)")[
        :, :, 2::3]
    v.tensor_mul(b["sq16"][:], mview, mview)
    v.reduce_sum(b["m_s"][:], mview, mybir.AxisListType.X)
    v.reduce_sum(b["cv_s"][:], cvview, mybir.AxisListType.X
                 ).then_inc(S["v"].h, 1); v.wait_ge(S["v"].h, sync_pts[0])
    v.reduce_sum(b["dm2"][:], b["sq16"][:], mybir.AxisListType.X)
    v.tensor_scalar(b["negmean"][:], b["m_s"][:], -1.0 / 16.0, None, ALU.mult)
    v.tensor_scalar(b["cv_s"][:], b["cv_s"][:], 1.0 / 1024.0, LN_EPS,
                    ALU.mult, ALU.add
                    ).then_inc(S["v"].h, 1); v.wait_ge(S["v"].h, sync_pts[1])
    v.tensor_mul(b["dm"][:], b["negmean"][:], b["negmean"][:])
    v.tensor_scalar(b["dm2"][:], b["dm2"][:], 1.0 / 16.0, None, ALU.mult
                    ).then_inc(S["v"].h, 1); v.wait_ge(S["v"].h, sync_pts[2])
    v.tensor_sub(b["dm2"][:], b["dm2"][:], b["dm"][:]
                 ).then_inc(S["v"].h, 1); v.wait_ge(S["v"].h, sync_pts[3])
    v.tensor_add(b["varpe"][:], b["dm2"][:], b["cv_s"][:]
                 ).then_inc(S["v"].h, 1)


def emit_ln(p: Prog, prev_v=0, prev_g=0, mix_l0ffn=False):
    b, S = p.b, p.S

    pend = p.pending_stats
    p.pending_stats = None
    if pend is None:
        v_sync_pts = [S["v"].bump() for _ in range(7)]
        def stats(v, pw=prev_v, pgg=prev_g):
            if pw:
                v.wait_ge(S["v"].h, pw)
            if pgg:
                v.wait_ge(S["g"].h, pgg)
            for T in range(T_TILES):
                v.bn_stats(b["bn6a"][:, T, 0:6], b["x"][:, T, 0:512])
                last = v.bn_stats(b["bn6a"][:, T, 6:12],
                                  b["x"][:, T, 512:1024])
            last.then_inc(S["v"].h, 1); v.wait_ge(S["v"].h, v_sync_pts[0])
            stats_combine(v, b, S, v_sync_pts[1:6])
        p.emit("vector", stats)
        v_stats = v_sync_pts[6]
    else:
        v_sync_pts = [S["v"].bump() for _ in range(6)]
        def stats(v, pw=pend[-1]):
            v.wait_ge(S["v"].h, pw)
            stats_combine(v, b, S, v_sync_pts[0:5])
        p.emit("vector", stats)
        v_stats = v_sync_pts[5]

    s_sqrt = S["s"].bump()
    def sqrt_phase(s, w=v_stats):
        s.wait_ge(S["v"].h, w)
        s.activation(b["sqv"][:], b["varpe"][:], AF.Sqrt).then_inc(S["s"].h, 1)
    p.emit("scalar", sqrt_phase)

    v_rstd = S["v"].bump()
    def rstd_phase(v, w=s_sqrt):
        v.wait_ge(S["s"].h, w)
        v.reciprocal(b["rstd"][:], b["sqv"][:]
                     ).then_inc(S["v"].h, 1); v.wait_ge(S["v"].h, v_rstd)
    p.emit("vector", rstd_phase)

    v_apply = [None] * T_TILES
    v_evac = [None] * T_TILES
    v_mixB = [None] * T_TILES
    v_mix = [None] * T_TILES

    m_b = [[None, None] for _ in range(T_TILES)]
    m_a = [[None, None, None] for _ in range(T_TILES)]

    def resv(T_e):
        v_evac[T_e] = S["v"].bump()
        if mix_l0ffn:
            m_b[T_e][0] = S["v"].bump()
            m_b[T_e][1] = S["v"].bump()
            v_mixB[T_e] = S["v"].bump()
            if T_e >= 1:
                m_a[T_e - 1][0] = S["v"].bump()
                m_a[T_e - 1][1] = S["v"].bump()
                m_a[T_e - 1][2] = S["v"].bump()
                v_mix[T_e - 1] = S["v"].bump()

    v_apply[0] = S["v"].bump()
    v_apply[1] = S["v"].bump()
    for T in range(2, T_TILES):
        resv(T - 2)
        v_apply[T] = S["v"].bump()
    resv(T_TILES - 2)
    resv(T_TILES - 1)
    if mix_l0ffn:
        m_a[T_TILES - 1][0] = S["v"].bump()
        m_a[T_TILES - 1][1] = S["v"].bump()
        m_a[T_TILES - 1][2] = S["v"].bump()
        v_mix[T_TILES - 1] = S["v"].bump()
    t_tr = [S["t"].bump() for _ in range(T_TILES)]

    def dve_phase(v):
        xlnT = b["xlnT"]
        xkT3 = b["arena"][:].rearrange("p (c t) -> p c t", c=C_CHUNKS)
        tmp3 = b["tkbf"][:, 3, :].rearrange("p (c j) -> p c j", c=C_CHUNKS)
        bcol = b["stw"][:].rearrange("p a c -> p (a c)").bitcast(
            BF16)[:, 0:128].rearrange("p (c t) -> p c t", c=C_CHUNKS)
        mukB = b["muk"][:].unsqueeze(2).broadcast_to([128, 8, 128])
        ommkB = b["ommk"][:].unsqueeze(2).broadcast_to([128, 8, 128])
        dmuB = b["dmu"][:].unsqueeze(2).broadcast_to([128, 8, 128])

        def apply(T):
            if T >= 2:
                v.wait_ge(S["t"].h, t_tr[T - 2])
            v.tensor_scalar(b["xln_t"][:, T % 2, :], b["x"][:, T, :],
                            b["negmean"][:, T:T + 1], b["rstd"][:, T:T + 1],
                            ALU.add, ALU.mult).then_inc(S["v"].h, 1)

        def evac(T):
            v.wait_ge(S["t"].h, t_tr[T])
            pst = b["P2"] if T % 2 == 0 else b["P3"]
            v.tensor_copy(b["xlnT"][:, :, T * 128:(T + 1) * 128],
                          pst[:].rearrange("p (c t) -> p c t", c=C_CHUNKS)
                          ).then_inc(S["v"].h, 1)
            if not mix_l0ffn:
                return
            lo = T * 128
            cur = xlnT[:, :, lo:lo + 128]
            v.wait_ge(S["v"].h, v_evac[T])
            v.tensor_copy(bcol[:, :, T:T + 1], xlnT[:, :, lo + 127:lo + 128])
            v.tensor_tensor(xkT3[:, :, lo:lo + 128], cur, mukB,
                            ALU.mult).then_inc(S["v"].h, 1)
            if T == 0:
                sh_in = xlnT[:, :, 0:127]
                sh_tm = tmp3[:, :, 1:128]
                sh_xk = xkT3[:, :, 1:128]
            else:
                sh_in = xlnT[:, :, lo - 1:lo + 127]
                sh_tm = tmp3[:, :, 0:128]
                sh_xk = xkT3[:, :, lo:lo + 128]
            v.tensor_tensor(sh_tm, sh_in,
                            ommkB[:, :, 0:127] if T == 0 else ommkB,
                            ALU.mult).then_inc(S["v"].h, 1)
            v.wait_ge(S["v"].h, m_b[T][1])
            v.tensor_add(sh_xk, sh_xk, sh_tm).then_inc(S["v"].h, 1)
            if T >= 1:
                mix_a(v, T - 1)

        def mix_a(v, Ta):
            lp = Ta * 128
            v.wait_ge(S["v"].h, v_mixB[Ta + 1] if Ta + 1 < T_TILES
                      else v_mixB[Ta])
            if Ta == 0:
                v.tensor_sub(xlnT[:, :, 1:128], xlnT[:, :, 1:128],
                             xlnT[:, :, 0:127]).then_inc(S["v"].h, 1)
                v.memset(tmp3[:, 0, 0:1], 0.0).then_inc(S["v"].h, 1)
            else:
                v.tensor_sub(xlnT[:, :, lp + 1:lp + 128],
                             xlnT[:, :, lp + 1:lp + 128],
                             xlnT[:, :, lp:lp + 127]
                             ).then_inc(S["v"].h, 1)
                v.wait_ge(S["v"].h, m_a[Ta][0])
                v.tensor_sub(xlnT[:, :, lp:lp + 1], xlnT[:, :, lp:lp + 1],
                             bcol[:, :, Ta - 1:Ta]).then_inc(S["v"].h, 1)
            v.wait_ge(S["v"].h, m_a[Ta][1])
            v.tensor_tensor(tmp3[:, :, 0:128],
                            xlnT[:, :, lp:lp + 128], dmuB,
                            ALU.mult).then_inc(S["v"].h, 1)
            v.wait_ge(S["v"].h, m_a[Ta][2])
            v.tensor_add(xlnT[:, :, lp:lp + 128],
                         tmp3[:, :, 0:128],
                         xkT3[:, :, lp:lp + 128]).then_inc(S["v"].h, 1)

        apply(0); apply(1)
        for T in range(2, T_TILES):
            evac(T - 2)
            apply(T)
        evac(T_TILES - 2); evac(T_TILES - 1)
        if mix_l0ffn:
            mix_a(v, T_TILES - 1)
    p.emit("vector", dve_phase)

    def transpose_phase(t):
        for T in range(T_TILES):
            t.wait_ge(S["v"].h, v_apply[T])
            if T >= 2:
                t.wait_ge(S["v"].h, v_evac[T - 2])
            pst = b["P2"] if T % 2 == 0 else b["P3"]
            for c in range(C_CHUNKS):
                last = t.matmul(pst[:, c * 128:(c + 1) * 128],
                                b["xln_t"][:, T % 2, c * 128:(c + 1) * 128],
                                b["ident"][:], start=True, stop=True)
            last.then_inc(S["t"].h, 1)
    p.emit("tensor", transpose_phase)
    return dict(xlnT_ready=v_evac[-1], tr_last=t_tr[-1], v_evac=v_evac,
                v_mix=v_mix, v_mixB=v_mixB)


def emit_l0ffn(p: Prog, ln):
    b, S = p.b, p.S
    xlnT = b["xlnT"]
    xkT = b["arena"][:].rearrange("p (c t) -> p c t", c=C_CHUNKS)

    w_wr = S["w"].bump(); w_wk = S["w"].bump(); w_wv = S["w"].bump()
    gate = p.wslot_free
    def wdma(g, gates=tuple(gate), w0=w_wr - 16):
        wv = w0
        for slot, name, gv in ((0, "wr", gates[0]), (1, "wk", gates[1]),
                               (2, "wv", gates[2])):
            if gv:
                g.wait_ge(S["t"].h, gv)
            if wv:
                g.wait_ge(S["w"].h, wv)
            src = p.dram[name].rearrange("(c p) e -> p c e", p=128)
            g.dma_start(out=b["wbuf"][:, slot, :].rearrange(
                            "p (c e) -> p c e", c=C_CHUNKS),
                        in_=src).then_inc(S["w"].h, 16)
            wv += 16
    p.emit("gpsimd", wdma)

    v_mix = ln["v_mix"]
    xrT = xlnT

    NT2 = T_TILES + 2
    t_rk = [None] * T_TILES; t_tp = [None] * T_TILES; t_out = [None] * T_TILES
    v_kact = [None] * T_TILES; v_yt = [None] * T_TILES
    v_bn = [None] * T_TILES
    g_x = [None] * T_TILES
    s_relu = [None] * T_TILES; s_r = [None] * T_TILES
    s_ke = [None] * T_TILES
    for i in range(NT2):
        if i < T_TILES:
            t_rk[i] = S["t"].bump()
            s_relu[i] = S["s"].bump()
            s_r[i] = S["s"].bump()
            v_kact[i] = S["v"].bump()
        if 1 <= i and i - 1 < T_TILES:
            t_tp[i - 1] = S["t"].bump()
            s_ke[i - 1] = S["s"].bump()
        if 2 <= i:
            t_out[i - 2] = S["t"].bump()
            v_yt[i - 2] = S["v"].bump()
            g_x[i - 2] = S["g"].bump()
            if p.want_tail_stats:
                v_bn[i - 2] = S["v"].bump()

    def kbuf(T):
        if T % 2 == 0:
            return b["kact"], b["kactT"], b["rbuf"]
        return b["tkbf"][:, 0, :], b["tkbf"][:, 1, :], b["tkbf"][:, 2, :]

    def tiles_tensor(t, wv=v_mix, ww=(w_wr, w_wk, w_wv),
                     pg=p.prep_guards):
        t.wait_ge(S["w"].h, ww[2])
        if pg:
            t.wait_ge(S["x"].h, pg["x"])
        for i in range(NT2):
            T = i
            if T < T_TILES:
                t.wait_ge(S["v"].h, wv[T])
                if T >= 1:
                    t.wait_ge(S["s"].h, s_r[T - 1])
                    t.wait_ge(S["v"].h, v_kact[T - 1])
                for c in range(C_CHUNKS):
                    for h in range(2):
                        t.matmul(b["P0"][:, h * 512:(h + 1) * 512],
                                 xrT[:, c, T * 128:(T + 1) * 128],
                                 wslot(b, 0)[:, c, h * 512:(h + 1) * 512],
                                 start=(c == 0), stop=(c == C_CHUNKS - 1))
                for c in range(C_CHUNKS):
                    for h in range(2):
                        last = t.matmul(
                            b["P1"][:, h * 512:(h + 1) * 512],
                            xkT[:, c, T * 128:(T + 1) * 128],
                            wslot(b, 1)[:, c, h * 512:(h + 1) * 512],
                            start=(c == 0), stop=(c == C_CHUNKS - 1))
                last.then_inc(S["t"].h, 1)
            if 1 <= i and i - 1 < T_TILES:
                T1 = i - 1
                ka, kaT, rb = kbuf(T1)
                t.wait_ge(S["v"].h, v_kact[T1])
                if T1 >= 1:
                    t.wait_ge(S["s"].h, s_ke[T1 - 1])
                for c in range(C_CHUNKS):
                    last = t.matmul(b["P2"][:, c * 128:(c + 1) * 128],
                                    ka[:, c * 128:(c + 1) * 128],
                                    b["ident"][:], start=True, stop=True)
                last.then_inc(S["t"].h, 1)
            if 2 <= i:
                T2 = i - 2
                ka, kaT, rb = kbuf(T2)
                t.wait_ge(S["s"].h, s_ke[T2])
                if T2 >= 1:
                    t.wait_ge(S["g"].h, g_x[T2 - 1])
                for c in range(C_CHUNKS):
                    for h in range(2):
                        last = t.matmul(
                            b["P3"][:, h * 512:(h + 1) * 512],
                            kaT[:, c * 128:(c + 1) * 128],
                            wslot(b, 2)[:, c, h * 512:(h + 1) * 512],
                            start=(c == 0), stop=(c == C_CHUNKS - 1))
                last.then_inc(S["t"].h, 1)
    p.emit("tensor", tiles_tensor)

    def tiles_scalar(s, pg=p.prep_guards):
        if pg:
            s.wait_ge(S["x"].h, pg["x"])
        for i in range(NT2):
            if i < T_TILES:
                T = i
                _, _, rb = kbuf(T)
                s.wait_ge(S["t"].h, t_rk[T])
                s.activation(b["relu_t"][:], b["P1"][:],
                             AF.Relu).then_inc(S["s"].h, 1)
                if T >= 2:
                    s.wait_ge(S["g"].h, g_x[T - 2])
                s.activation(rb, b["P0"][:],
                             AF.Sigmoid).then_inc(S["s"].h, 1)
            if 1 <= i and i - 1 < T_TILES:
                T1 = i - 1
                ka, kaT, rb = kbuf(T1)
                s.wait_ge(S["t"].h, t_tp[T1])
                if T1 >= 1:
                    s.wait_ge(S["t"].h, t_out[T1 - 1])
                s.copy(kaT, b["P2"][:]).then_inc(S["s"].h, 1)
    p.emit("scalar", tiles_scalar)

    def tiles_vector(v, tst=p.want_tail_stats):
        for i in range(NT2):
            if i < T_TILES:
                T = i
                ka, kaT, rb = kbuf(T)
                v.wait_ge(S["s"].h, s_relu[T])
                if T >= 2:
                    v.wait_ge(S["t"].h, t_tp[T - 2])
                v.tensor_tensor(ka, b["relu_t"][:], b["P1"][:],
                                ALU.mult).then_inc(S["v"].h, 1)
            if 2 <= i:
                T2 = i - 2
                ka, kaT, rb = kbuf(T2)
                v.wait_ge(S["t"].h, t_out[T2])
                v.wait_ge(S["s"].h, s_r[T2])
                v.scalar_tensor_tensor(b["ytile"][:], b["P3"][:], 0.5, rb,
                                       ALU.mult, ALU.mult).then_inc(S["v"].h,
                                                                    1)
                if tst:
                    v.wait_ge(S["g"].h, g_x[T2])
                    v.bn_stats(b["bn6a"][:, T2, 0:6], b["x"][:, T2, 0:512])
                    v.bn_stats(b["bn6a"][:, T2, 6:12],
                               b["x"][:, T2, 512:1024]
                               ).then_inc(S["v"].h, 1)
    p.emit("vector", tiles_vector)

    def tiles_pool(g):
        for T2 in range(T_TILES):
            g.wait_ge(S["v"].h, v_yt[T2])
            g.tensor_add(b["x"][:, T2, :], b["x"][:, T2, :],
                         b["ytile"][:]).then_inc(S["g"].h, 1)
    p.emit("gpsimd", tiles_pool)
    if p.want_tail_stats:
        p.pending_stats = v_bn

    p.wslot_free = [t_rk[-1], t_rk[-1], t_out[-1]]
    return dict(done_v=(v_bn[-1] if p.want_tail_stats else v_yt[-1]),
                done_g=g_x[-1])


def emit_lnffn(p: Prog, prev_v=0, prev_g=0):
    b, S = p.b, p.S
    emit_stats(p, prev_v, prev_g)

    w_wr = S["w"].bump(); w_wk = S["w"].bump(); w_wv = S["w"].bump()
    gate = p.wslot_free
    def wdma(g, gates=tuple(gate), w0=w_wr - 16):
        wv = w0
        for slot, name, gv in ((0, "wr", gates[0]), (1, "wk", gates[1]),
                               (2, "wv", gates[2])):
            if gv:
                g.wait_ge(S["t"].h, gv)
            if wv:
                g.wait_ge(S["w"].h, wv)
            src = p.dram[name].rearrange("(c p) e -> p c e", p=128)
            g.dma_start(out=b["wbuf"][:, slot, :].rearrange(
                            "p (c e) -> p c e", c=C_CHUNKS),
                        in_=src).then_inc(S["w"].h, 16)
            wv += 16
    p.emit("gpsimd", wdma)

    NS = T_TILES + 6
    tst = p.want_tail_stats
    v_a = [None] * T_TILES
    t_tr = [None] * T_TILES; s_e = [None] * T_TILES
    m_b = [None] * T_TILES; m_a = [None] * T_TILES
    t_rkr = [None] * T_TILES; t_rkk = [None] * T_TILES
    t_tp = [None] * T_TILES; t_out = [None] * T_TILES
    s_r = [None] * T_TILES; s_relu = [None] * T_TILES; s_ke = [None] * T_TILES
    v_kact = [None] * T_TILES; v_yt = [None] * T_TILES; v_bn = [None] * T_TILES
    g_x = [None] * T_TILES
    for i in range(NS):
        if i < T_TILES:
            v_a[i] = S["v"].bump()
        if 0 <= i - 2 < T_TILES:
            m_b[i - 2] = [S["v"].bump() for _ in range(3)]
        if 0 <= i - 3 < T_TILES:
            m_a[i - 3] = [S["v"].bump() for _ in range(4)]
        if 0 <= i - 5 < T_TILES:
            v_kact[i - 5] = S["v"].bump()
        if 0 <= i - 6 < T_TILES:
            v_yt[i - 6] = S["v"].bump()
            if tst:
                v_bn[i - 6] = S["v"].bump()
        if 0 <= i - 1 < T_TILES:
            t_tr[i - 1] = S["t"].bump()
        if 0 <= i - 4 < T_TILES:
            t_rkr[i - 4] = S["t"].bump()
        if 0 <= i - 6 < T_TILES:
            t_out[i - 6] = S["t"].bump()
        if 0 <= i - 5 < T_TILES:
            t_tp[i - 5] = S["t"].bump()
        if 0 <= i - 4 < T_TILES:
            t_rkk[i - 4] = S["t"].bump()
        if 0 <= i - 1 < T_TILES:
            s_e[i - 1] = S["s"].bump()
        if 0 <= i - 6 < T_TILES:
            s_ke[i - 6] = S["s"].bump()
        if 0 <= i - 4 < T_TILES:
            s_r[i - 4] = S["s"].bump()
            s_relu[i - 4] = S["s"].bump()
        if 0 <= i - 6 < T_TILES:
            g_x[i - 6] = S["g"].bump()

    def kbuf(T):
        if T % 2 == 0:
            return b["kact"], b["kactT"], b["rbuf"]
        return b["tkbf"][:, 0, :], b["tkbf"][:, 1, :], b["tkbf"][:, 2, :]

    def dve(v, pg=p.prep_guards):
        if pg:
            v.wait_ge(S["x"].h, pg["x"])
            v.wait_ge(S["t"].h, pg["t"])
        xlnT = b["xlnT"]
        xkT3 = b["arena"][:].rearrange("p (c t) -> p c t", c=C_CHUNKS)
        tmp3 = b["tkbf"][:, 3, :].rearrange("p (c j) -> p c j", c=C_CHUNKS)
        bcol = b["stw"][:].rearrange("p a c -> p (a c)").bitcast(
            BF16)[:, 0:128].rearrange("p (c t) -> p c t", c=C_CHUNKS)
        mukB = b["muk"][:].unsqueeze(2).broadcast_to([128, 8, 128])
        ommkB = b["ommk"][:].unsqueeze(2).broadcast_to([128, 8, 128])
        dmuB = b["dmu"][:].unsqueeze(2).broadcast_to([128, 8, 128])
        for i in range(NS):
            if i < T_TILES:
                T = i
                if T >= 2:
                    v.wait_ge(S["t"].h, t_tr[T - 2])
                v.tensor_scalar(aslot(b, T % 2),
                                b["x"][:, T, :].rearrange("p (c j) -> p c j",
                                                          c=C_CHUNKS),
                                b["negmean"][:, T:T + 1], b["rstd"][:, T:T + 1],
                                ALU.add, ALU.mult).then_inc(S["v"].h, 1)
            if 0 <= i - 2 < T_TILES:
                T = i - 2
                lo = T * 128
                cur = xlnT[:, :, lo:lo + 128]
                v.wait_ge(S["s"].h, s_e[T])
                if T >= 2:
                    v.wait_ge(S["v"].h, m_a[T - 2][3])
                elif T == 1:
                    v.wait_ge(S["v"].h, m_b[0][2])
                v.tensor_copy(bcol[:, :, T:T + 1], xlnT[:, :, lo + 127:lo + 128])
                v.tensor_tensor(xkT3[:, :, lo:lo + 128], cur, mukB,
                                ALU.mult).then_inc(S["v"].h, 1)
                if T == 0:
                    sh_in = xlnT[:, :, 0:127]
                    sh_tm = tmp3[:, :, 1:128]
                    sh_xk = xkT3[:, :, 1:128]
                else:
                    sh_in = xlnT[:, :, lo - 1:lo + 127]
                    sh_tm = tmp3[:, :, 0:128]
                    sh_xk = xkT3[:, :, lo:lo + 128]
                v.tensor_tensor(sh_tm, sh_in,
                                ommkB[:, :, 0:127] if T == 0 else ommkB,
                                ALU.mult).then_inc(S["v"].h, 1)
                v.wait_ge(S["v"].h, m_b[T][1])
                v.tensor_add(sh_xk, sh_xk, sh_tm).then_inc(S["v"].h, 1)
            if 0 <= i - 3 < T_TILES:
                Ta = i - 3
                lp = Ta * 128
                v.wait_ge(S["v"].h, m_b[Ta + 1][2] if Ta + 1 < T_TILES
                          else m_b[Ta][2])
                if Ta == 0:
                    v.tensor_sub(xlnT[:, :, 1:128], xlnT[:, :, 1:128],
                                 xlnT[:, :, 0:127]).then_inc(S["v"].h, 1)
                    v.memset(tmp3[:, 0, 0:1], 0.0).then_inc(S["v"].h, 1)
                else:
                    v.tensor_sub(xlnT[:, :, lp + 1:lp + 128],
                                 xlnT[:, :, lp + 1:lp + 128],
                                 xlnT[:, :, lp:lp + 127]
                                 ).then_inc(S["v"].h, 1)
                    v.wait_ge(S["v"].h, m_a[Ta][0])
                    v.tensor_sub(xlnT[:, :, lp:lp + 1], xlnT[:, :, lp:lp + 1],
                                 bcol[:, :, Ta - 1:Ta]).then_inc(S["v"].h, 1)
                v.wait_ge(S["v"].h, m_a[Ta][1])
                v.tensor_tensor(tmp3[:, :, 0:128],
                                xlnT[:, :, lp:lp + 128], dmuB,
                                ALU.mult).then_inc(S["v"].h, 1)
                v.wait_ge(S["v"].h, m_a[Ta][2])
                v.tensor_add(xlnT[:, :, lp:lp + 128],
                             tmp3[:, :, 0:128],
                             xkT3[:, :, lp:lp + 128]).then_inc(S["v"].h, 1)
            if 0 <= i - 5 < T_TILES:
                T = i - 5
                ka, kaT, rb = kbuf(T)
                v.wait_ge(S["s"].h, s_relu[T])
                if T >= 2:
                    v.wait_ge(S["t"].h, t_tp[T - 2])
                v.tensor_tensor(ka, b["relu_t"][:], b["relu_t"][:],
                                ALU.mult).then_inc(S["v"].h, 1)
            if 0 <= i - 6 < T_TILES:
                T = i - 6
                ka, kaT, rb = kbuf(T)
                v.wait_ge(S["t"].h, t_out[T])
                v.wait_ge(S["s"].h, s_r[T])
                if T >= 1:
                    v.wait_ge(S["g"].h, g_x[T - 1])
                v.scalar_tensor_tensor(b["ytile"][:], b["P3"][:], 0.5, rb,
                                       ALU.mult, ALU.mult).then_inc(S["v"].h, 1)
                if tst:
                    v.wait_ge(S["g"].h, g_x[T])
                    v.bn_stats(b["bn6a"][:, T, 0:6], b["x"][:, T, 0:512])
                    v.bn_stats(b["bn6a"][:, T, 6:12],
                               b["x"][:, T, 512:1024]
                               ).then_inc(S["v"].h, 1)
    p.emit("vector", dve)

    def sca(s, pg=p.prep_guards):
        if pg:
            s.wait_ge(S["x"].h, pg["x"])
        xlnT = b["xlnT"]
        PLN3 = b["P1"][:].rearrange("p (c t) -> p c t", c=C_CHUNKS)
        for i in range(NS):
            if 0 <= i - 1 < T_TILES:
                T = i - 1
                s.wait_ge(S["t"].h, t_tr[T])
                s.copy(xlnT[:, :, T * 128:(T + 1) * 128],
                       PLN3).then_inc(S["s"].h, 1)
            if 0 <= i - 6 < T_TILES:
                T = i - 6
                ka, kaT, rb = kbuf(T)
                s.wait_ge(S["t"].h, t_tp[T])
                if T >= 2:
                    s.wait_ge(S["t"].h, t_out[T - 2])
                s.copy(kaT, b["P2"][:]).then_inc(S["s"].h, 1)
            if 0 <= i - 4 < T_TILES:
                T = i - 4
                ka, kaT, rb = kbuf(T)
                s.wait_ge(S["t"].h, t_rkr[T])
                if T >= 2:
                    s.wait_ge(S["v"].h, v_yt[T - 2])
                s.activation(rb, b["P0"][:],
                             AF.Sigmoid).then_inc(S["s"].h, 1)
                s.wait_ge(S["t"].h, t_rkk[T])
                if T >= 1:
                    s.wait_ge(S["v"].h, v_kact[T - 1])
                s.activation(b["relu_t"][:], b["P0"][:],
                             AF.Relu).then_inc(S["s"].h, 1)
    p.emit("scalar", sca)

    def ten(t, ww=(w_wr, w_wk, w_wv), pg=p.prep_guards):
        xlnT = b["xlnT"]
        xkT = b["arena"][:].rearrange("p (c t) -> p c t", c=C_CHUNKS)
        for i in range(NS):
            if 0 <= i - 1 < T_TILES:
                T = i - 1
                t.wait_ge(S["v"].h, v_a[T])
                if T >= 1:
                    t.wait_ge(S["s"].h, s_e[T - 1])
                asl = aslot(b, T % 2)
                for c in range(C_CHUNKS):
                    last = t.matmul(b["P1"][:, c * 128:(c + 1) * 128],
                                    asl[:, c, :],
                                    b["ident"][:], start=True, stop=True)
                last.then_inc(S["t"].h, 1)
            if 0 <= i - 4 < T_TILES:
                T = i - 4
                if T == 0:
                    t.wait_ge(S["w"].h, ww[0])
                    if pg:
                        t.wait_ge(S["x"].h, pg["x"])
                t.wait_ge(S["v"].h, m_a[T][3])
                if T >= 1:
                    t.wait_ge(S["s"].h, s_relu[T - 1])
                for c in range(C_CHUNKS):
                    for h in range(2):
                        last = t.matmul(
                            b["P0"][:, h * 512:(h + 1) * 512],
                            xlnT[:, c, T * 128:(T + 1) * 128],
                            wslot(b, 0)[:, c, h * 512:(h + 1) * 512],
                            start=(c == 0), stop=(c == C_CHUNKS - 1))
                last.then_inc(S["t"].h, 1)
            if 0 <= i - 6 < T_TILES:
                T = i - 6
                ka, kaT, rb = kbuf(T)
                if T == 0:
                    t.wait_ge(S["w"].h, ww[2])
                t.wait_ge(S["s"].h, s_ke[T])
                if T >= 1:
                    t.wait_ge(S["v"].h, v_yt[T - 1])
                for c in range(C_CHUNKS):
                    for h in range(2):
                        last = t.matmul(
                            b["P3"][:, h * 512:(h + 1) * 512],
                            kaT[:, c * 128:(c + 1) * 128],
                            wslot(b, 2)[:, c, h * 512:(h + 1) * 512],
                            start=(c == 0), stop=(c == C_CHUNKS - 1))
                last.then_inc(S["t"].h, 1)
            if 0 <= i - 5 < T_TILES:
                T = i - 5
                ka, kaT, rb = kbuf(T)
                t.wait_ge(S["v"].h, v_kact[T])
                if T >= 1:
                    t.wait_ge(S["s"].h, s_ke[T - 1])
                for c in range(C_CHUNKS):
                    last = t.matmul(b["P2"][:, c * 128:(c + 1) * 128],
                                    ka[:, c * 128:(c + 1) * 128],
                                    b["ident"][:], start=True, stop=True)
                last.then_inc(S["t"].h, 1)
            if 0 <= i - 4 < T_TILES:
                T = i - 4
                if T == 0:
                    t.wait_ge(S["w"].h, ww[1])
                t.wait_ge(S["s"].h, s_r[T])
                t.wait_ge(S["v"].h, m_b[T][2])
                for c in range(C_CHUNKS):
                    for h in range(2):
                        last = t.matmul(
                            b["P0"][:, h * 512:(h + 1) * 512],
                            xkT[:, c, T * 128:(T + 1) * 128],
                            wslot(b, 1)[:, c, h * 512:(h + 1) * 512],
                            start=(c == 0), stop=(c == C_CHUNKS - 1))
                last.then_inc(S["t"].h, 1)
    p.emit("tensor", ten)

    def pool(g):
        for T in range(T_TILES):
            g.wait_ge(S["v"].h, v_yt[T])
            g.tensor_add(b["x"][:, T, :], b["x"][:, T, :],
                         b["ytile"][:]).then_inc(S["g"].h, 1)
    p.emit("gpsimd", pool)

    if tst:
        p.pending_stats = v_bn
    p.wslot_free = [t_rkr[-1], t_rkk[-1], t_out[-1]]
    return dict(done_v=(v_bn[-1] if tst else v_yt[-1]),
                done_g=g_x[-1])


def emit_l0attn(p: Prog, prev_v=0, prev_g=0):
    b, S = p.b, p.S
    emit_stats(p, prev_v, prev_g)
    xlnT = b["xlnT"]
    qhT = b["arena"][:].rearrange("p (c t) -> p c t", c=C_CHUNKS)
    W = p.dram
    GB = 4

    w_q = S["w"].bump(); w_k0 = S["w"].bump(); w_v0 = S["w"].bump()
    NSQ = T_TILES + 5
    v_a = [None] * T_TILES; t_tr = [None] * T_TILES; s_e = [None] * T_TILES
    t_q = [None] * T_TILES; t_qt = [None] * T_TILES
    s_q = [None] * T_TILES; s_q2 = [None] * T_TILES
    v_q1 = [None] * T_TILES; v_q2 = [None] * T_TILES; v_qc = [None] * T_TILES
    for i in range(NSQ):
        if i < T_TILES:
            v_a[i] = S["v"].bump()
        if 0 <= i - 5 < T_TILES:
            v_qc[i - 5] = S["v"].bump()
        if 0 <= i - 3 < T_TILES:
            v_q1[i - 3] = S["v"].bump()
            v_q2[i - 3] = S["v"].bump()
        if 0 <= i - 1 < T_TILES:
            t_tr[i - 1] = S["t"].bump()
        if 0 <= i - 3 < T_TILES:
            t_q[i - 3] = S["t"].bump()
        if 0 <= i - 4 < T_TILES:
            t_qt[i - 4] = S["t"].bump()
        if 0 <= i - 1 < T_TILES:
            s_e[i - 1] = S["s"].bump()
        if 0 <= i - 3 < T_TILES:
            s_q[i - 3] = S["s"].bump()
            s_q2[i - 3] = S["s"].bump()
    s_qsq = S["s"].bump()
    v_qeps = S["v"].bump(); v_qrk = S["v"].bump()
    kv = []
    for kk in range(2):
        d = {k: [None] * T_TILES for k in
             ("t_k", "s_k", "v_k1", "v_k3", "t_v", "v_vs", "t_st")}
        d["s_sq"] = [None] * (T_TILES // GB)
        d["v_eps"] = [None] * (T_TILES // GB)
        d["v_rk"] = [None] * (T_TILES // GB)
        d["s_k2"] = [None] * T_TILES
        for T in range(GB):
            d["t_k"][T] = S["t"].bump()
        for T in range(T_TILES):
            d["t_v"][T] = S["t"].bump()
            if T + GB < T_TILES:
                d["t_k"][T + GB] = S["t"].bump()
            d["t_st"][T] = S["t"].bump()
        for g in range(T_TILES // GB):
            for T in range(g * GB, (g + 1) * GB):
                d["s_k"][T] = S["s"].bump()
                d["s_k2"][T] = S["s"].bump()
            d["s_sq"][g] = S["s"].bump()
        for T in range(GB):
            d["v_k1"][T] = S["v"].bump()
            d["v_k3"][T] = S["v"].bump()
        for T in range(T_TILES):
            if T % GB == 0:
                d["v_eps"][T // GB] = S["v"].bump()
                d["v_rk"][T // GB] = S["v"].bump()
            d["v_vs"][T] = S["v"].bump()
            if T + GB < T_TILES:
                d["v_k1"][T + GB] = S["v"].bump()
                d["v_k3"][T + GB] = S["v"].bump()
        kv.append(d)
    w_k1 = S["w"].bump(); w_v1 = S["w"].bump(); w_wo = S["w"].bump()
    w_lnbc = S["w"].bump()
    v_st0 = S["v"].bump(); v_stm = S["v"].bump()
    v_stz = S["v"].bump(); v_stb = S["v"].bump()
    t_biasmm = S["t"].bump()
    t_bB1 = S["t"].bump()
    t_rowsum = S["t"].bump()
    t_bB2 = S["t"].bump()
    v_biasp = S["v"].bump()
    v_scalewo = S["v"].bump()
    v_bB1 = S["v"].bump()
    v_rowrow = S["v"].bump()
    v_bB2 = S["v"].bump()
    t_o1 = []; t_o2 = []; v_o1 = []; v_o2 = []; v_o3 = []
    for T in range(T_TILES):
        t_o1.append(S["t"].bump()); t_o2.append(S["t"].bump())
        v_o1.append(S["v"].bump()); v_o2.append(S["v"].bump())
        v_o3.append(S["v"].bump())
    v_oc = [S["v"].bump() for _ in range(6)]
    s_osq = S["s"].bump()
    v_orstd = S["v"].bump()
    v_onmr = S["v"].bump()
    v_w1 = []; g_w2 = []; g_w3 = []; t_wo = []; v_bn = []
    s_w2r = []
    for T in range(T_TILES):
        t_wo.append(S["t"].bump())
        v_w1.append(S["v"].bump())
        g_w2.append(S["g"].bump()); g_w3.append(S["g"].bump())
        s_w2r.append(S["s"].bump())
        if p.want_tail_stats:
            v_bn.append(S["v"].bump())

    gates0 = list(p.wslot_free)

    def wdma(g, w0=w_q - 16):
        plan = [(0, "wq0", gates0[0]), (1, "wk00", gates0[1]),
                (2, "wv00", gates0[2]),
                (0, "wk01", t_q[-1]), (1, "wv01", kv[0]["t_k"][-1]),
                (2, "wo0", kv[0]["t_v"][-1])]
        if STOP_AFTER <= 1:
            plan = plan[:1]
        elif STOP_AFTER <= 2:
            plan = plan[:3]
        wv = w0
        for slot, name, gv in plan:
            if gv:
                g.wait_ge(S["t"].h, gv)
            if wv:
                g.wait_ge(S["w"].h, wv)
            g.dma_start(out=b["wbuf"][:, slot, :].rearrange(
                            "p (c e) -> p c e", c=C_CHUNKS),
                        in_=W[name].rearrange("(c p) e -> p c e", p=128)
                        ).then_inc(S["w"].h, 16)
            wv += 16
        g.wait_ge(S["w"].h, wv)
        g.dma_start(out=b["lnbc"][:], in_=W["lnbc"]).then_inc(S["w"].h, 16)
    p.emit("gpsimd", wdma)

    def proj_tile(t, pst, slot, T, bias_part, pack):
        for c in range(C_CHUNKS):
            for h in range(2):
                t.matmul(pst[:, h * 512:(h + 1) * 512],
                         xlnT[:, c, T * 128:(T + 1) * 128],
                         wslot(b, slot)[:, c, h * 512:(h + 1) * 512],
                         start=(c == 0), stop=False)
        for c in range(C_CHUNKS):
            t.matmul(pst[:, c * 128:(c + 1) * 128],
                     xlnT[:, c, T * 128:(T + 1) * 128],
                     b["ident"][:], start=False, stop=False)
        for h in range(2):
            last = t.matmul(pst[:, h * 512:(h + 1) * 512],
                            b["onesbf"][bias_part:bias_part + 1, :],
                            b[pack][bias_part:bias_part + 1,
                                    h * 512:(h + 1) * 512],
                            start=False, stop=True)
        return last

    def q_tensor(t):
        for i in range(NSQ):
            if 0 <= i - 1 < T_TILES:
                T = i - 1
                t.wait_ge(S["v"].h, v_a[T])
                if T >= 1:
                    t.wait_ge(S["s"].h, s_e[T - 1])
                asl = aslot(b, T % 2)
                for c in range(C_CHUNKS):
                    last = t.matmul(b["P1"][:, c * 128:(c + 1) * 128],
                                    asl[:, c, :],
                                    b["ident"][:], start=True, stop=True)
                last.then_inc(S["t"].h, 1)
            if 0 <= i - 3 < T_TILES:
                T = i - 3
                if T == 0:
                    t.wait_ge(S["w"].h, w_q)
                t.wait_ge(S["s"].h, s_e[T])
                if T >= 1:
                    t.wait_ge(S["s"].h, s_q[T - 1])
                last = proj_tile(t, b["P0"], 0, T, 0, "bpk0")
                last.then_inc(S["t"].h, 1)
            if 0 <= i - 4 < T_TILES:
                T = i - 4
                t.wait_ge(S["v"].h, v_q1[T])
                if T >= 1:
                    t.wait_ge(S["v"].h, v_qc[T - 1])
                for c in range(C_CHUNKS):
                    last = t.matmul(b["P3"][:, c * 128:(c + 1) * 128],
                                    b["bfA"][:, c * 128:(c + 1) * 128],
                                    b["ident"][:], start=True, stop=True)
                last.then_inc(S["t"].h, 1)
    p.emit("tensor", q_tensor)

    def q_scalar(s):
        PLN3 = b["P1"][:].rearrange("p (c t) -> p c t", c=C_CHUNKS)
        for i in range(NSQ):
            if 0 <= i - 1 < T_TILES:
                T = i - 1
                s.wait_ge(S["t"].h, t_tr[T])
                s.copy(xlnT[:, :, T * 128:(T + 1) * 128],
                       PLN3).then_inc(S["s"].h, 1)
            if 0 <= i - 3 < T_TILES:
                T = i - 3
                s.wait_ge(S["t"].h, t_q[T])
                if T >= 1:
                    s.wait_ge(S["v"].h, v_q1[T - 1])
                    s.wait_ge(S["s"].h, s_q2[T - 1])
                s.activation(b["fA"][:], b["P0"][:],
                             AF.Silu).then_inc(S["s"].h, 1)
                if T >= 1:
                    s.wait_ge(S["v"].h, v_q2[T - 1])
                s.activation(b["fB"][:], b["fA"][:], AF.Square,
                             bias=b["halfc"][:]).then_inc(S["s"].h, 1)
        s.wait_ge(S["v"].h, v_q2[T_TILES - 1])
        s.activation(b["ssq_s"][:], b["ssq"][:],
                     AF.Sqrt).then_inc(S["s"].h, 1)
    p.emit("scalar", q_scalar)

    def q_vector(v, pg=p.prep_guards):
        if pg:
            v.wait_ge(S["x"].h, pg["x"])
            v.wait_ge(S["t"].h, pg["t"])
        for i in range(NSQ):
            if i < T_TILES:
                T = i
                if T == 0:
                    v.memset(b["halfc"][:], 0.5)
                if T >= 2:
                    v.wait_ge(S["t"].h, t_tr[T - 2])
                v.tensor_scalar(aslot(b, T % 2),
                                b["x"][:, T, :].rearrange("p (c j) -> p c j",
                                                          c=C_CHUNKS),
                                b["negmean"][:, T:T + 1], b["rstd"][:, T:T + 1],
                                ALU.add, ALU.mult).then_inc(S["v"].h, 1)
            if 0 <= i - 5 < T_TILES:
                T = i - 5
                v.wait_ge(S["t"].h, t_qt[T])
                v.tensor_copy(qhT[:, :, T * 128:(T + 1) * 128],
                              b["P3"][:].rearrange("p (c t) -> p c t",
                                                   c=C_CHUNKS)
                              ).then_inc(S["v"].h, 1)
            if 0 <= i - 3 < T_TILES:
                T = i - 3
                v.wait_ge(S["s"].h, s_q[T])
                if T >= 1:
                    v.wait_ge(S["t"].h, t_qt[T - 1])
                v.tensor_scalar(b["bfA"][:], b["fA"][:], 0.5, None,
                                ALU.add).then_inc(S["v"].h, 1)
                v.wait_ge(S["s"].h, s_q2[T])
                v.reduce_sum(b["ssq"][:, T, :],
                             b["fB"][:].rearrange("p (h d) -> p h d", d=64),
                             mybir.AxisListType.X).then_inc(S["v"].h, 1)
        v.wait_ge(S["s"].h, s_qsq)
        v.tensor_scalar(b["ssq_s"][:], b["ssq_s"][:], 1e-6, None,
                        ALU.add).then_inc(S["v"].h, 1)
        v.wait_ge(S["v"].h, v_qeps)
        v.reciprocal(b["rq"][:], b["ssq_s"][:]).then_inc(S["v"].h, 1)
    p.emit("vector", q_vector)


    if STOP_AFTER <= 1:
        return dict(done_v=v_qrk)
    for kk in range(2):
        d = kv[kk]
        kslot, vslot = (1, 2) if kk == 0 else (0, 1)
        w_kw = w_k0 if kk == 0 else w_k1
        w_vw = w_v0 if kk == 0 else w_v1
        kb = 32 if kk == 0 else 64
        vb = 0 if kk == 0 else 32
        kpack = "bpk0"
        vpack = "bpk1"

        def kv_tensor(t, d=d, kslot=kslot, vslot=vslot, w_kw=w_kw,
                      w_vw=w_vw, kk=kk, kb=kb, vb=vb, vpack=vpack):
            t.wait_ge(S["w"].h, w_kw)

            def kproj(T):
                if kk == 0 and T == 0:
                    t.wait_ge(S["s"].h, s_q[T_TILES - 1])
                elif T >= 1 or kk == 1:
                    prev = d["s_k"][T - 1] if T >= 1 else kv[0]["s_k"][-1]
                    t.wait_ge(S["s"].h, prev)
                last = proj_tile(t, b["P0"], kslot, T, kb, kpack)
                last.then_inc(S["t"].h, 1)

            for T in range(GB):
                kproj(T)
            t.wait_ge(S["w"].h, w_vw)
            for T in range(T_TILES):
                if kk == 1 or T >= 1:
                    prev = (d["v_vs"][T - 1] if T >= 1
                            else kv[0]["v_vs"][-1])
                    t.wait_ge(S["v"].h, prev)
                last = proj_tile(t, b["P1"], vslot, T, vb, vpack)
                last.then_inc(S["t"].h, 1)
                if T + GB < T_TILES:
                    kproj(T + GB)
                t.wait_ge(S["v"].h, d["v_vs"][T])
                pst = b["P2"] if kk == 0 else b["P3"]
                for bp in range(8):
                    last = t.matmul(
                        pst[:, bp * 128:(bp + 1) * 128],
                        b["tkbf"][:, T % GB, bp * 128:(bp + 1) * 128],
                        b["bfA"][:, bp * 128:(bp + 1) * 128],
                        start=(T == 0 and bp in (0, 4)),
                        stop=(T == T_TILES - 1 and bp in (3, 7)))
                last.then_inc(S["t"].h, 1)
        p.emit("tensor", kv_tensor)

        def kv_scalar(s, d=d, kk=kk):
            for g in range(T_TILES // GB):
                for T in range(g * GB, (g + 1) * GB):
                    s.wait_ge(S["t"].h, d["t_k"][T])
                    if T >= 1:
                        s.wait_ge(S["v"].h, d["v_k3"][T - 1])
                    elif kk == 0:
                        s.wait_ge(S["v"].h, v_qrk)
                    else:
                        s.wait_ge(S["v"].h, kv[0]["v_vs"][-1])
                    s.activation(b["fA"][:], b["P0"][:],
                                 AF.Silu).then_inc(S["s"].h, 1)
                    s.wait_ge(S["v"].h, d["v_k1"][T])
                    s.activation(b["fA"][:], b["fB"][:],
                                 AF.Square).then_inc(S["s"].h, 1)
                s.wait_ge(S["v"].h, d["v_k3"][(g + 1) * GB - 1])
                s.activation(b["ssk_s"][:], b["ssk"][:],
                             AF.Sqrt).then_inc(S["s"].h, 1)
        p.emit("scalar", kv_scalar)

        def kv_vector(v, d=d, kk=kk, pg=p.prep_guards):
            if pg:
                v.wait_ge(S["t"].h, pg["t"])

            def k13(T):
                g = T // GB
                v.wait_ge(S["s"].h, d["s_k"][T])
                if T >= GB or kk == 1:
                    pv = (d["t_st"][T - GB] if T >= GB
                          else kv[0]["t_st"][T - GB + T_TILES])
                    v.wait_ge(S["t"].h, pv)
                v.tensor_scalar(b["fB"][:], b["fA"][:], 0.5, None,
                                ALU.add)
                v.tensor_scalar(b["tkbf"][:, T % GB, :], b["fA"][:], 0.5,
                                None, ALU.add).then_inc(S["v"].h, 1)
                v.wait_ge(S["s"].h, d["s_k2"][T])
                if g >= 1:
                    v.wait_ge(S["s"].h, d["s_sq"][g - 1])
                v.reduce_sum(b["ssk"][:, T % GB, :],
                             b["fA"][:].rearrange("p (h d) -> p h d",
                                                  d=64),
                             mybir.AxisListType.X).then_inc(S["v"].h, 1)

            for T in range(GB):
                k13(T)
            for T in range(T_TILES):
                g = T // GB
                if T % GB == 0:
                    v.wait_ge(S["s"].h, d["s_sq"][g])
                    v.tensor_scalar(b["ssk_s"][:], b["ssk_s"][:], 1e-6,
                                    None, ALU.add).then_inc(S["v"].h, 1)
                    v.wait_ge(S["v"].h, d["v_eps"][g])
                    v.reciprocal(b["rk"][:], b["ssk_s"][:]
                                 ).then_inc(S["v"].h, 1)
                v.wait_ge(S["v"].h, d["v_rk"][g])
                v.wait_ge(S["t"].h, d["t_v"][T])
                if T >= 1 or kk == 1:
                    pv = (d["t_st"][T - 1] if T >= 1
                          else kv[0]["t_st"][-1])
                    v.wait_ge(S["t"].h, pv)
                v.tensor_tensor(
                    b["bfA"][:].rearrange("p (h d) -> p h d", d=64),
                    b["P1"][:].rearrange("p (h d) -> p h d", d=64),
                    b["rk"][:, T % GB, :].unsqueeze(2).broadcast_to(
                        [128, 16, 64]),
                    ALU.mult).then_inc(S["v"].h, 1)
                if T + GB < T_TILES:
                    k13(T + GB)
        p.emit("vector", kv_vector)
        if STOP_AFTER <= 2 + kk:
            p.wslot_free = [0, 0, 0]
            return dict(done_v=d["v_vs"][-1])


    def state_vec(v):
        v.wait_ge(S["t"].h, kv[1]["t_st"][-1])
        d2 = lambda ps, half: ps[:].rearrange(
            "p (bl two e) -> p bl two e", two=2, e=64)[
            half * 64:half * 64 + 64, :, half, :]
        stA0 = b["fA"][0:64, 512:1024].rearrange("p (bl e) -> p bl e", e=64)
        stA1 = b["fA"][64:128, 512:1024].rearrange("p (bl e) -> p bl e", e=64)
        v.tensor_copy(stA0, d2(b["P2"], 0))
        v.tensor_copy(stA1, d2(b["P2"], 1)).then_inc(S["v"].h, 1)
        v.wait_ge(S["v"].h, v_st0)
        v.tensor_mul(b["fA"][0:64, 0:512].rearrange("p (bl e) -> p bl e",
                                                    e=64),
                     stA0, d2(b["P3"], 0))
        v.tensor_mul(b["fA"][64:128, 0:512].rearrange("p (bl e) -> p bl e",
                                                      e=64),
                     stA1, d2(b["P3"], 1)).then_inc(S["v"].h, 1)
        v.wait_ge(S["v"].h, v_stm)
        v.memset(b["stateb"], 0.0).then_inc(S["v"].h, 1)
        v.wait_ge(S["v"].h, v_stz)
        sb3 = b["stateb"].rearrange("p (bl e) -> p bl e", e=128)
        s4 = 1.0 / (2048.0 * 2048.0)
        v.tensor_scalar(sb3[0:64, :, 0:64],
                        b["fA"][0:64, 0:512].rearrange("p (bl e) -> p bl e",
                                                       e=64),
                        s4, None, ALU.mult)
        v.tensor_scalar(sb3[64:128, :, 64:128],
                        b["fA"][64:128, 0:512].rearrange("p (bl e) -> p bl e",
                                                         e=64),
                        s4, None, ALU.mult).then_inc(S["v"].h, 1)
    p.emit("vector", state_vec)

    def prep_tensor(t):
        t.wait_ge(S["w"].h, w_lnbc)
        t.wait_ge(S["v"].h, v_stb)
        for c in range(C_CHUNKS):
            for h in range(2):
                t.matmul(b["P3"][0:1, h * 512:(h + 1) * 512],
                         b["lnbc"][:, c:c + 1],
                         wslot(b, 2)[:, c, h * 512:(h + 1) * 512],
                         start=(c == 0), stop=False)
        for h in range(2):
            last = t.matmul(b["P3"][0:1, h * 512:(h + 1) * 512],
                            b["onesbf"][64:65, 0:1],
                            b["bpk1"][64:65, h * 512:(h + 1) * 512],
                            start=False, stop=True)
        last.then_inc(S["t"].h, 1)
        t.wait_ge(S["v"].h, v_biasp)
        for h in range(2):
            last = t.matmul(b["P3"][:, h * 512:(h + 1) * 512],
                            b["onesbf"][0:1, :],
                            b["rowtmp"][0:1, h * 512:(h + 1) * 512],
                            start=True, stop=True)
        last.then_inc(S["t"].h, 1)
        t.wait_ge(S["v"].h, v_bB1)
        t.wait_ge(S["v"].h, v_scalewo)
        for c in range(C_CHUNKS):
            for h in range(2):
                last = t.matmul(b["P3"][0:1, h * 512:(h + 1) * 512],
                                b["onesbf"][:, 0:1],
                                wslot(b, 2)[:, c, h * 512:(h + 1) * 512],
                                start=(c == 0), stop=(c == C_CHUNKS - 1))
        last.then_inc(S["t"].h, 1)
        t.wait_ge(S["v"].h, v_rowrow)
        for h in range(2):
            last = t.matmul(b["P3"][:, h * 512:(h + 1) * 512],
                            b["onesbf"][0:1, :],
                            b["rowtmp"][0:1, h * 512:(h + 1) * 512],
                            start=True, stop=True)
        last.then_inc(S["t"].h, 1)
    p.emit("tensor", prep_tensor)

    def prep_vector(v):
        v.wait_ge(S["t"].h, t_biasmm)
        v.tensor_copy(b["rowtmp"][0:1, :], b["P3"][0:1, :]
                      ).then_inc(S["v"].h, 1)
        v.wait_ge(S["v"].h, v_biasp)
        for c in range(C_CHUNKS):
            last = v.tensor_scalar(wslot(b, 2)[:, c, :], wslot(b, 2)[:, c, :],
                                   b["lngc"][:, c:c + 1], None, ALU.mult)
        last.then_inc(S["v"].h, 1)
        v.wait_ge(S["t"].h, t_bB1)
        v.tensor_copy(b["biaspB"], b["P3"][:]).then_inc(S["v"].h, 1)
        v.wait_ge(S["t"].h, t_rowsum)
        v.tensor_copy(b["rowtmp"][0:1, :], b["P3"][0:1, :]
                      ).then_inc(S["v"].h, 1)
        v.wait_ge(S["t"].h, t_bB2)
        v.tensor_copy(b["rowsumB"], b["P3"][:]).then_inc(S["v"].h, 1)
    p.emit("vector", prep_vector)

    if STOP_AFTER <= 4:
        return dict(done_v=v_bB2)

    def o1_tensor(t):
        t.wait_ge(S["v"].h, v_stb)
        t.wait_ge(S["v"].h, v_bB2)
        for T in range(T_TILES):
            if T >= 1:
                t.wait_ge(S["v"].h, v_o1[T - 1])
            sb3 = b["stateb"].rearrange("p (bl e) -> p bl e", e=128)
            for c in range(C_CHUNKS):
                last = t.matmul(b["P1"][:, c * 128:(c + 1) * 128],
                                qhT[:, c, T * 128:(T + 1) * 128],
                                sb3[:, c, :],
                                start=(c == 0 or c == 4),
                                stop=(c == 3 or c == 7))
            last.then_inc(S["t"].h, 1)
            t.wait_ge(S["v"].h, v_o1[T])
            if T >= 2:
                t.wait_ge(S["v"].h, v_o3[T - 2])
            for c in range(C_CHUNKS):
                last = t.matmul(b["P3"][:, c * 128:(c + 1) * 128],
                                b["bfA"][:, c * 128:(c + 1) * 128],
                                b["ident"][:], start=True, stop=True)
            last.then_inc(S["t"].h, 1)
    p.emit("tensor", o1_tensor)

    def o1_vector(v):
        v.wait_ge(S["v"].h, v_qrk)
        v.wait_ge(S["v"].h, v_bB2)
        for T in range(T_TILES):
            v.wait_ge(S["t"].h, t_o1[T])
            if T >= 1:
                v.wait_ge(S["t"].h, t_o2[T - 1])
                v.wait_ge(S["v"].h, v_o2[T - 1])
            v.tensor_tensor(
                b["bfA"][:].rearrange("p (h d) -> p h d", d=64),
                b["P1"][:].rearrange("p (h d) -> p h d", d=64),
                b["rq"][:, T, :].unsqueeze(2).broadcast_to([128, 16, 64]),
                ALU.mult).then_inc(S["v"].h, 1)
            v.wait_ge(S["v"].h, v_o1[T])
            v.bn_stats(b["bn6a"][:, T, 0:6], b["bfA"][:, 0:512])
            v.bn_stats(b["bn6a"][:, T, 6:12], b["bfA"][:, 512:1024]
                       ).then_inc(S["v"].h, 1)
            v.wait_ge(S["t"].h, t_o2[T])
            v.tensor_copy(qhT[:, :, T * 128:(T + 1) * 128],
                          b["P3"][:].rearrange("p (c t) -> p c t",
                                               c=C_CHUNKS)
                          ).then_inc(S["v"].h, 1)
        v.wait_ge(S["v"].h, v_o2[T_TILES - 1])
        stats_combine(v, b, S, v_oc[0:5])
    p.emit("vector", o1_vector)

    def o_scalar(s):
        s.wait_ge(S["v"].h, v_oc[5])
        s.activation(b["sqv"][:], b["varpe"][:],
                     AF.Sqrt).then_inc(S["s"].h, 1)
    p.emit("scalar", o_scalar)

    def o_rstd(v):
        v.wait_ge(S["s"].h, s_osq)
        v.reciprocal(b["rstd"][:], b["sqv"][:]).then_inc(S["v"].h, 1)
        v.wait_ge(S["v"].h, v_orstd)
        v.tensor_mul(b["nmr"][:], b["negmean"][:], b["rstd"][:]
                     ).then_inc(S["v"].h, 1)
    p.emit("vector", o_rstd)

    if STOP_AFTER <= 5:
        return dict(done_v=v_onmr)

    def o2_tensor(t):
        t.wait_ge(S["v"].h, v_onmr)
        for T in range(T_TILES):
            if T >= 1:
                t.wait_ge(S["v"].h, v_w1[T - 1])
            for c in range(C_CHUNKS):
                for h in range(2):
                    last = t.matmul(b["P0"][:, h * 512:(h + 1) * 512],
                                    qhT[:, c, T * 128:(T + 1) * 128],
                                    wslot(b, 2)[:, c, h * 512:(h + 1) * 512],
                                    start=(c == 0), stop=(c == C_CHUNKS - 1))
            last.then_inc(S["t"].h, 1)
    p.emit("tensor", o2_tensor)

    def o2_vector(v, tst=p.want_tail_stats):
        for T in range(T_TILES):
            v.wait_ge(S["t"].h, t_wo[T])
            v.scalar_tensor_tensor(b["x"][:, T, :], b["P0"][:],
                                   b["rstd"][:, T:T + 1], b["x"][:, T, :],
                                   ALU.mult, ALU.add).then_inc(S["v"].h, 1)
            if tst:
                v.wait_ge(S["g"].h, g_w3[T])
                v.bn_stats(b["bn6a"][:, T, 0:6], b["x"][:, T, 0:512])
                v.bn_stats(b["bn6a"][:, T, 6:12], b["x"][:, T, 512:1024]
                           ).then_inc(S["v"].h, 1)
    p.emit("vector", o2_vector)

    s_w2 = s_w2r
    def o2_scalar2(s):
        s.wait_ge(S["v"].h, v_onmr)
        for T in range(T_TILES):
            if T >= 1:
                s.wait_ge(S["g"].h, g_w2[T - 1])
            s.activation(b["ytile"][:], b["rowsumB"][:], AF.Copy,
                         scale=b["nmr"][:, T:T + 1]).then_inc(S["s"].h, 1)
    p.emit("scalar", o2_scalar2)

    def o2_pool(g):
        for T in range(T_TILES):
            g.wait_ge(S["v"].h, v_w1[T])
            g.wait_ge(S["s"].h, s_w2[T])
            g.tensor_add(b["x"][:, T, :], b["x"][:, T, :], b["ytile"][:]
                         ).then_inc(S["g"].h, 1)
            g.wait_ge(S["g"].h, g_w2[T])
            g.tensor_add(b["x"][:, T, :], b["x"][:, T, :], b["biaspB"][:]
                         ).then_inc(S["g"].h, 1)
    p.emit("gpsimd", o2_pool)
    if p.want_tail_stats:
        p.pending_stats = v_bn

    p.wslot_free = [t_wo[-1], kv[1]["t_st"][-1], t_wo[-1]]
    return dict(done_v=(v_bn[-1] if p.want_tail_stats else v_w1[-1]),
                done_g=g_w3[-1])




def emit_l1prep(p: Prog):
    b, S = p.b, p.S
    W = p.dram

    w_rm = S["w"].bump(); w_rup = S["w"].bump()
    def prep_wdma(g, w0=w_rm - 16):
        if w0:
            g.wait_ge(S["w"].h, w0)
        g.dma_start(out=b["tkbf"][:].rearrange("p a b -> p (a b)").rearrange(
                        "p (hr i) -> p hr i", i=16)[0:16, :, :],
                    in_=W["rmt"].rearrange("h r j i -> j (h r) i")
                    ).then_inc(S["w"].h, 16)
        g.wait_ge(S["w"].h, w0 + 16)
        g.dma_start(out=b["arena"][:].rearrange("p (hr k) -> p hr k", k=64)[
                        0:16, :, :],
                    in_=W["rupt"].rearrange("h r j k -> j (h r) k")
                    ).then_inc(S["w"].h, 16)
    p.emit("gpsimd", prep_wdma)

    t_wave = [S["t"].bump() for _ in range(16)]
    v_wave = [S["v"].bump() for _ in range(16)]
    x_raw = [S["x"].bump() for _ in range(16)]

    rmt3 = b["tkbf"][:].rearrange("p a b -> p (a b)").rearrange(
        "p (hr i) -> p hr i", i=16)
    rup3 = b["arena"][:].rearrange("p (hr k) -> p hr k", k=64)

    def prep_tensor(t, ww=S["w"].n):
        t.wait_ge(S["w"].h, ww)
        for w in range(16):
            if w >= 1:
                t.wait_ge(S["v"].h, v_wave[w - 1])
            for o in range(16):
                hr = w * 16 + o
                last = t.matmul(b["P0"][0:16, o * 64:(o + 1) * 64],
                                rmt3[0:16, hr, :], rup3[0:16, hr, :],
                                start=True, stop=True)
            last.then_inc(S["t"].h, 1)
    p.emit("tensor", prep_tensor)

    def prep_vector(v):
        for w in range(16):
            v.wait_ge(S["t"].h, t_wave[w])
            if w >= 2:
                v.wait_ge(S["x"].h, x_raw[w - 2])
            v.tensor_copy(b["bfA"][0:16, :] if w % 2 == 0 else
                          b["bfA"][32:48, :],
                          b["P0"][0:16, :]).then_inc(S["v"].h, 1)
    p.emit("vector", prep_vector)

    def prep_store(sy):
        for w in range(16):
            sy.wait_ge(S["v"].h, v_wave[w])
            sy.wait_ge(S["x"].h, x_raw[w] - 16)
            srcv = b["bfA"][0:16, :] if w % 2 == 0 else b["bfA"][32:48, :]
            sy.dma_start(out=W["meffraw"].rearrange(
                             "(w i e) -> w i e", w=16, i=16)[w],
                         in_=srcv).then_inc(S["x"].h, 16)
    p.emit("sync", prep_store)

    v_zero = S["v"].bump()
    def zero_vec(v):
        v.wait_ge(S["x"].h, x_raw[-1])
        v.memset(b["bfA"][:], 0.0).then_inc(S["v"].h, 1)
    p.emit("vector", zero_vec)
    x_zero = [S["x"].bump() for _ in range(16)]
    n_place = 8 * 16 * 2
    x_place = [S["pl"].bump() for _ in range(n_place)]

    def place(sy):
        sy.wait_ge(S["v"].h, v_zero)
        sy.wait_ge(S["x"].h, x_raw[-1])
        ms = W["meffscratch"].rearrange("(g p e) -> g p e", g=8, p=128)
        for g in range(8):
            sy.wait_ge(S["x"].h, x_zero[2 * g] - 16)
            sy.dma_start(out=ms[g][:, 0:1024],
                         in_=b["bfA"][:]).then_inc(S["x"].h, 16)
            sy.wait_ge(S["x"].h, x_zero[2 * g + 1] - 16)
            sy.dma_start(out=ms[g][:, 1024:2048],
                         in_=b["bfA"][:]).then_inc(S["x"].h, 16)
        sy.wait_ge(S["x"].h, x_zero[-1])
        raw4 = W["meffraw"].rearrange("(w i o k) -> w i o k", w=16, i=16,
                                      o=16)
        for g in range(8):
            for tile in range(16):
                h, rp = tile // 4, tile % 4
                for rl in range(2):
                    r = 8 * g + 2 * rp + rl
                    hr = h * 64 + r
                    dst = ms[g][32 * rp + 16 * rl:32 * rp + 16 * rl + 16,
                                tile * 128 + 64 * rl:tile * 128 + 64 * rl + 64]
                    sy.dma_start(out=dst,
                                 in_=raw4[hr // 16, :, hr % 16, :]
                                 ).then_inc(S["pl"].h, 16)
    p.emit("sync", place)
    p.prep_guards = dict(t=t_wave[-1], x=x_zero[-1])
    return dict(meff_ready=x_place[-1])


def emit_l1ffn(p: Prog, ln, prep):
    b, S = p.b, p.S
    xlnT = b["xlnT"]
    hT = b["arena"][:].rearrange("p (f e) -> p f e", f=16)
    mfA = b["xln_t"][:].rearrange("p a b -> p (a b)").rearrange(
        "p (f e) -> p f e", e=128)
    mfB = b["fA"][:].bitcast(BF16).rearrange("p (f e) -> p f e", e=128)
    mf_of = lambda g: mfA if g % 2 == 0 else mfB
    W = p.dram
    NG, NF, NTH = 8, 16, 2

    sprev, vprev, gprev = S["s"].n, S["v"].n, S["g"].n
    NGT = NG * NTH
    w_meff = [None] * NG
    w_half = [[None, None] for _ in range(NG)]
    t_rails = [[None] * NF for _ in range(NGT)]
    s_gelu = [[None] * NF for _ in range(NGT)]
    t_cproj = [[None] * 8 for _ in range(NGT)]
    v_flush = [[None] * 8 for _ in range(NGT)]
    v_bn = [None] * T_TILES
    for g in range(NG):
        w_meff[g] = S["w"].bump()
        w_half[g][0] = S["w"].bump(2)
        w_half[g][1] = S["w"].bump(2)
    for gt in range(NGT):
        g = gt // NTH
        for f in range(NF):
            s_gelu[gt][f] = S["s"].bump()
        for tt in range(8):
            v_flush[gt][tt] = S["v"].bump()
            if p.want_tail_stats and g == NG - 1:
                v_bn[(gt % NTH) * 8 + tt] = S["v"].bump()
    t_cp7h = [None] * NGT
    for f in range(10):
        t_rails[0][f] = S["t"].bump()
    for gt in range(NGT):
        for f in range(10, NF):
            t_rails[gt][f] = S["t"].bump()
        t_cproj[gt][0] = S["t"].bump()
        for tt in range(1, 8):
            if tt == 7:
                t_cp7h[gt] = S["t"].bump()
            t_cproj[gt][tt] = S["t"].bump()
            if 1 <= tt <= 5 and gt + 1 < NGT:
                t_rails[gt + 1][2 * (tt - 1)] = S["t"].bump()
                t_rails[gt + 1][2 * (tt - 1) + 1] = S["t"].bump()

    gates0 = list(p.wslot_free)

    def slot_of(g, half):
        return (g * 2 + half) % 3

    def wdma(g_eng, w0=w_meff[0] - 16):
        ms = W["meffscratch"].rearrange("(g p e) -> g p e", g=8, p=128)
        w2 = W["w2t"].rearrange("(h gg f p) e -> h gg f p e", h=4, gg=8, f=4)
        wv = w0
        for g in range(NG):
            g_eng.wait_ge(S["pl"].h, prep["meff_ready"])
            if wv:
                g_eng.wait_ge(S["w"].h, wv)
            if g <= 1:
                g_eng.wait_ge(S["t"].h, ln["tr_last"])
                g_eng.wait_ge(S["v"].h, ln["xlnT_ready"])
            else:
                g_eng.wait_ge(S["t"].h, t_rails[(g - 2) * NTH + 1][NF - 1])
            g_eng.dma_start(out=mf_of(g), in_=ms[g].rearrange(
                                "p (f e) -> p f e", e=128)
                            ).then_inc(S["w"].h, 16)
            wv += 16
            for half in range(2):
                hh = g * 2 + half
                slot = slot_of(g, half)
                if hh >= 3:
                    gp, hp = divmod(hh - 3, 2)
                    g_eng.wait_ge(S["t"].h, t_cproj[gp * NTH + 1][7])
                else:
                    gv = gates0[slot]
                    if gv:
                        g_eng.wait_ge(S["t"].h, gv)
                for h4 in range(2):
                    hb = half * 2 + h4
                    g_eng.wait_ge(S["w"].h, wv)
                    g_eng.dma_start(
                        out=b["wbuf"][:, slot, :].rearrange(
                            "p (f e) -> p f e", f=8)[:, h4 * 4:(h4 + 1) * 4,
                                                     :],
                        in_=w2[hb, g, :, :, :].rearrange("f p e -> p f e")
                        ).then_inc(S["w"].h, 16)
                    wv += 16
    p.emit("gpsimd", wdma)

    Bsl = [b["tkbf"][:, 0, :], b["tkbf"][:, 1, :], b["tkbf"][:, 2, :],
           b["tkbf"][:, 3, :],
           b["fB"][:].bitcast(BF16)[:, 0:1024],
           b["fB"][:].bitcast(BF16)[:, 1024:2048],
           b["bfA"][:], b["bpk1"][:]]

    def hfull(gt, f):
        if gt % 2 == 1 and f < 8:
            return Bsl[f]
        return hT[:, f, :]

    def ffn_tensor(t, w=ln["xlnT_ready"]):
        t.wait_ge(S["v"].h, w)

        def rails_one(gt, f):
            g, th = divmod(gt, NTH)
            if f == 0 and th == 0:
                t.wait_ge(S["w"].h, w_meff[g])
            psz = b["P2"] if f % 2 == 0 else b["P3"]
            if f >= 2:
                t.wait_ge(S["s"].h, s_gelu[gt][f - 2])
            elif gt >= 1:
                t.wait_ge(S["s"].h, s_gelu[gt - 1][NF - 2 + f])
            for nn in range(2):
                last = t.matmul(
                    psz[:, nn * 512:(nn + 1) * 512],
                    mf_of(g)[:, f, :],
                    xlnT[:, g, th * 1024 + nn * 512:
                         th * 1024 + (nn + 1) * 512],
                    start=True, stop=True)
            last.then_inc(S["t"].h, 1)

        def cproj_chunk(gt, tt, f0, f1, first=None, fin=None, mark=False):
            g, th = divmod(gt, NTH)
            if first is None:
                first = (f0 == 0)
            if fin is None:
                fin = (f1 == NF)
            pso = b["P0"] if tt % 2 == 0 else b["P1"]
            for f in range(f0, f1):
                if tt == 0:
                    t.wait_ge(S["s"].h, s_gelu[gt][f])
                if th == 0 and tt == 0 and f == 8:
                    t.wait_ge(S["w"].h, w_half[g][1] - 16)
                if th == 0 and tt == 0 and f == 12:
                    t.wait_ge(S["w"].h, w_half[g][1])
                w2s = b["wbuf"][:, slot_of(g, f // 8), :].rearrange(
                    "p (f2 e) -> p f2 e", f2=8)
                src = hfull(gt, f)[:, tt * 128:(tt + 1) * 128]
                for hh2 in range(2):
                    last = t.matmul(
                        pso[:, hh2 * 512:(hh2 + 1) * 512],
                        src,
                        w2s[:, f % 8, hh2 * 512:(hh2 + 1) * 512],
                        start=(f == f0 and first), stop=(f == f1 - 1 and fin))
            if mark:
                last.then_inc(S["t"].h, 1)

        for f in range(10):
            rails_one(0, f)
        for gt in range(NGT):
            g, th = divmod(gt, NTH)
            if th == 0:
                t.wait_ge(S["w"].h, w_half[g][0])
            if gt >= 1:
                t.wait_ge(S["v"].h, v_flush[gt - 1][6])
            rails_one(gt, 10); rails_one(gt, 11)
            cproj_chunk(gt, 0, 0, 8)
            rails_one(gt, 12); rails_one(gt, 13)
            cproj_chunk(gt, 0, 8, 12)
            rails_one(gt, 14); rails_one(gt, 15)
            cproj_chunk(gt, 0, 12, NF, mark=True)
            for tt in range(1, 8):
                t.wait_ge(S["s"].h, s_gelu[gt][NF - 1])
                if tt >= 2:
                    t.wait_ge(S["v"].h, v_flush[gt][tt - 2])
                elif gt >= 1:
                    t.wait_ge(S["v"].h, v_flush[gt - 1][7])
                if tt == 7:
                    cproj_chunk(gt, 7, 8, NF, first=True, fin=False,
                                mark=True)
                    cproj_chunk(gt, 7, 0, 8, first=False, fin=True,
                                mark=True)
                else:
                    cproj_chunk(gt, tt, 0, NF, mark=True)
                if 1 <= tt <= 5 and gt + 1 < NGT:
                    rails_one(gt + 1, 2 * (tt - 1))
                    rails_one(gt + 1, 2 * (tt - 1) + 1)
    p.emit("tensor", ffn_tensor)

    def ffn_scalar(s):
        s.wait_ge(S["s"].h, sprev)
        s.wait_ge(S["v"].h, vprev)
        s.wait_ge(S["g"].h, gprev)
        for gt in range(NGT):
            for f in range(NF):
                s.wait_ge(S["t"].h, t_rails[gt][f])
                if f < 8:
                    if gt >= 2:
                        s.wait_ge(S["t"].h, t_cproj[gt - 2][7])
                else:
                    if gt >= 1:
                        s.wait_ge(S["t"].h, t_cp7h[gt - 1])
                psz = b["P2"] if f % 2 == 0 else b["P3"]
                s.activation(hfull(gt, f), psz[:],
                             AF.Gelu).then_inc(S["s"].h, 1)
    p.emit("scalar", ffn_scalar)

    def ffn_vector(v, tst=p.want_tail_stats):
        for g in range(NG):
            for th in range(NTH):
                gt = g * NTH + th
                for tt in range(8):
                    T = th * 8 + tt
                    pso = b["P0"] if tt % 2 == 0 else b["P1"]
                    v.wait_ge(S["t"].h, t_cproj[gt][tt])
                    v.scalar_tensor_tensor(
                        b["x"][:, T, :], pso[:], 0.5, b["x"][:, T, :],
                        ALU.mult, ALU.add).then_inc(S["v"].h, 1)
                    if tst and g == NG - 1:
                        v.wait_ge(S["v"].h, v_flush[gt][tt])
                        v.bn_stats(b["bn6a"][:, T, 0:6],
                                   b["x"][:, T, 0:512])
                        v.bn_stats(b["bn6a"][:, T, 6:12],
                                   b["x"][:, T, 512:1024]
                                   ).then_inc(S["v"].h, 1)
    p.emit("vector", ffn_vector)
    if p.want_tail_stats:
        p.pending_stats = v_bn

    last_cp = t_cproj[(NG - 1) * NTH + 1][7]
    p.wslot_free = [last_cp, last_cp, last_cp]
    p.l1ffn_meff_last = t_rails[(NG - 1) * NTH + 1][NF - 1]
    return dict(done_v=v_flush[NG * NTH - 1][7])


def emit_l1attn(p: Prog, ln):
    b, S = p.b, p.S
    xlnT = b["xlnT"]
    arena3 = b["arena"][:].rearrange("p (c t) -> p c t", c=C_CHUNKS)
    W = p.dram
    gB = b["tkbf"][:, 2, :]
    bB = b["tkbf"][:, 3, :]
    vt2 = b["tkbf"][:, 0:2, :].rearrange("p a b -> p (a b)")

    w_q = S["w"].bump(); w_v = S["w"].bump(); w_gb = S["w"].bump()
    w_lam = S["w"].bump(); w_g = S["w"].bump(); w_wo = S["w"].bump()
    gates0 = list(p.wslot_free)
    t_gb = S["t"].bump()
    t_vch = [S["t"].bump() for _ in range(C_CHUNKS)]
    t_q = [None] * T_TILES; t_str = [None] * T_TILES
    for T in range(T_TILES):
        t_str[T] = S["t"].bump(); t_q[T] = S["t"].bump()
    t_g = [None] * T_TILES; t_y3t = [None] * T_TILES; t_wo = [None] * T_TILES
    t_g[0] = S["t"].bump()
    for T in range(T_TILES):
        t_y3t[T] = S["t"].bump()
        if T + 1 < T_TILES:
            t_g[T + 1] = S["t"].bump()
        t_wo[T] = S["t"].bump()
    v_gb = S["v"].bump()
    s_vch = [None] * C_CHUNKS; v_scan = [None] * C_CHUNKS
    for ec in range(C_CHUNKS):
        s_vch[ec] = S["s"].bump()
    for ec in range(C_CHUNKS):
        v_scan[ec] = S["v"].bump()
    s_sty = [S["s"].bump() for _ in range(T_TILES)]
    v_yst = [None] * T_TILES
    v_yss = [None] * T_TILES
    for T in range(T_TILES):
        v_yst[T] = S["v"].bump()
        v_yss[T] = S["v"].bump()
    v_oc = [S["v"].bump() for _ in range(5)]
    v_rstd = S["v"].bump()
    v_bn = [None] * T_TILES
    v_p = [[None, None] for _ in range(T_TILES)]
    g_p = []
    v_d0 = [None] * T_TILES
    for T in range(T_TILES):
        g_p.append([None] + [S["g"].bump() for _ in range(3)])
    v_d0[0] = S["v"].bump(); v_d0[1] = S["v"].bump()
    for T in range(T_TILES):
        v_p[T][0] = S["v"].bump()
        v_p[T][1] = S["v"].bump()
        if p.want_tail_stats:
            v_bn[T] = S["v"].bump()
        if T + 2 < T_TILES:
            v_d0[T + 2] = S["v"].bump()
    s_sq = S["s"].bump()
    s_gate = [S["s"].bump() for _ in range(T_TILES)]

    def wdma(g, w0=w_q - 16):
        plan = [(0, "wq1", gates0[0]), (1, "wv1", gates0[1]),
                (None, "gbpack", 0), ("lam", "lamc", 0),
                (2, "wg1", gates0[2]), (1, "wo1", t_vch[-1])]
        wv = w0
        for slot, name, gv in plan:
            if gv:
                g.wait_ge(S["t"].h, gv)
            if wv:
                g.wait_ge(S["w"].h, wv)
            if slot == "lam":
                g.dma_start(out=b["lamc"][:], in_=W["lamc"]
                            ).then_inc(S["w"].h, 16)
            elif slot is None:
                g.dma_start(out=b["gbp"][:], in_=W["gbpack"]
                            ).then_inc(S["w"].h, 16)
            else:
                g.dma_start(out=b["wbuf"][:, slot, :].rearrange(
                                "p (c e) -> p c e", c=C_CHUNKS),
                            in_=W[name].rearrange("(c p) e -> p c e", p=128)
                            ).then_inc(S["w"].h, 16)
            wv += 16
    p.emit("gpsimd", wdma)

    def attn_tensor1(t, w=ln["xlnT_ready"]):
        t.wait_ge(S["v"].h, w)
        t.wait_ge(S["w"].h, w_gb)
        for hh in range(2):
            t.matmul(b["P2"][:, hh * 512:(hh + 1) * 512],
                     b["onesbf"][0:1, :],
                     b["gbp"][0:1, hh * 512:(hh + 1) * 512],
                     start=True, stop=True)
        for hh in range(2):
            last = t.matmul(b["P3"][:, hh * 512:(hh + 1) * 512],
                            b["onesbf"][32:33, :],
                            b["gbp"][32:33, hh * 512:(hh + 1) * 512],
                            start=True, stop=True)
        last.then_inc(S["t"].h, 1)
        t.wait_ge(S["v"].h, w)
        t.wait_ge(S["w"].h, w_v)
        for ec in range(C_CHUNKS):
            if ec >= 1:
                t.wait_ge(S["s"].h, s_vch[ec - 1])
            for c in range(C_CHUNKS):
                for tq in range(4):
                    psv = b["P0"] if tq < 2 else b["P1"]
                    last = t.matmul(
                        psv[:, (tq % 2) * 512:(tq % 2) * 512 + 512],
                        wslot(b, 1)[:, c, ec * 128:(ec + 1) * 128],
                        xlnT[:, c, tq * 512:(tq + 1) * 512],
                        start=(c == 0), stop=(c == C_CHUNKS - 1))
            last.then_inc(S["t"].h, 1)
    p.emit("tensor", attn_tensor1)

    def attn_sc1(s):
        for ec in range(C_CHUNKS):
            s.wait_ge(S["t"].h, t_vch[ec])
            if ec >= 1:
                s.wait_ge(S["v"].h, v_scan[ec - 1])
            s.copy(vt2[:, 0:1024], b["P0"][:])
            s.copy(vt2[:, 1024:2048], b["P1"][:]
                   ).then_inc(S["s"].h, 1)
    p.emit("scalar", attn_sc1)

    def attn_vec1(v):
        v.wait_ge(S["t"].h, t_gb)
        v.tensor_copy(gB, b["P2"][:])
        v.tensor_copy(bB, b["P3"][:]).then_inc(S["v"].h, 1)
        v.wait_ge(S["w"].h, w_lam)
        for ec in range(C_CHUNKS):
            v.wait_ge(S["s"].h, s_vch[ec])
            v.tensor_tensor_scan(
                arena3[:, ec, :],
                b["lamc"][:, ec:ec + 1].broadcast_to([128, 2048]),
                vt2[:], 0.0, ALU.mult, ALU.add).then_inc(S["v"].h, 1)
    p.emit("vector", attn_vec1)

    def attn_tensor2(t):
        t.wait_ge(S["w"].h, w_q)
        t.wait_ge(S["v"].h, v_scan[-1])
        for T in range(T_TILES):
            if T >= 1:
                t.wait_ge(S["s"].h, s_sty[T - 1])
            for c in range(C_CHUNKS):
                last = t.matmul(b["P3"][:, c * 128:(c + 1) * 128],
                                arena3[:, c, T * 128:(T + 1) * 128],
                                b["ident"][:], start=True, stop=True)
            last.then_inc(S["t"].h, 1)
            if T >= 1:
                t.wait_ge(S["v"].h, v_yst[T - 1])
            for c in range(C_CHUNKS):
                for hh in range(2):
                    last = t.matmul(
                        b["P2"][:, hh * 512:(hh + 1) * 512],
                        xlnT[:, c, T * 128:(T + 1) * 128],
                        wslot(b, 0)[:, c, hh * 512:(hh + 1) * 512],
                        start=(c == 0), stop=(c == C_CHUNKS - 1))
            last.then_inc(S["t"].h, 1)
    p.emit("tensor", attn_tensor2)

    def attn_sc_sty(s):
        for T in range(T_TILES):
            s.wait_ge(S["t"].h, t_str[T])
            if T >= 1:
                s.wait_ge(S["v"].h, v_yst[T - 1])
            s.copy(b["kact"][:], b["P3"][:]).then_inc(S["s"].h, 1)
    p.emit("scalar", attn_sc_sty)

    def attn_vec2(v):
        for T in range(T_TILES):
            v.wait_ge(S["s"].h, s_sty[T])
            v.wait_ge(S["t"].h, t_q[T])
            yv = arena3[:, :, T * 128:(T + 1) * 128]
            v.tensor_tensor(yv,
                            b["P2"][:].rearrange("p (c j) -> p c j",
                                                 c=C_CHUNKS),
                            b["kact"][:].rearrange("p (c j) -> p c j",
                                                   c=C_CHUNKS),
                            ALU.mult).then_inc(S["v"].h, 1)
            v.wait_ge(S["v"].h, v_yst[T])
            for gg in range(8):
                last = v.bn_stats(b["bn6b"][:, T, gg, :], yv[:, gg, :])
            last.then_inc(S["v"].h, 1)
        v.wait_ge(S["v"].h, v_yss[-1])
        stats_combine16(v, b, S, v_oc[0:4])
    p.emit("vector", attn_vec2)

    def attn_scalar(s):
        s.wait_ge(S["v"].h, v_oc[4])
        s.activation(b["sqv"][:], b["varpe"][:],
                     AF.Sqrt).then_inc(S["s"].h, 1)
        for T in range(T_TILES):
            s.wait_ge(S["t"].h, t_g[T])
            if T >= 1:
                s.wait_ge(S["g"].h, g_p[T - 1][3])
            s.activation(b["bfA"][:], b["P2"][:],
                         AF.Silu).then_inc(S["s"].h, 1)
    p.emit("scalar", attn_scalar)

    def attn_rstd(v):
        v.wait_ge(S["s"].h, s_sq)
        v.reciprocal(b["rstd"][:], b["sqv"][:]).then_inc(S["v"].h, 1)
    p.emit("vector", attn_rstd)

    def attn_tensor3(t):
        def gproj(T):
            if T >= 1:
                t.wait_ge(S["s"].h, s_gate[T - 1])
            for c in range(C_CHUNKS):
                for hh in range(2):
                    last = t.matmul(
                        b["P2"][:, hh * 512:(hh + 1) * 512],
                        xlnT[:, c, T * 128:(T + 1) * 128],
                        wslot(b, 2)[:, c, hh * 512:(hh + 1) * 512],
                        start=(c == 0), stop=(c == C_CHUNKS - 1))
            last.then_inc(S["t"].h, 1)

        t.wait_ge(S["v"].h, v_rstd)
        t.wait_ge(S["w"].h, w_wo)
        gproj(0)
        for T in range(T_TILES):
            t.wait_ge(S["g"].h, g_p[T][3])
            if T >= 2:
                t.wait_ge(S["v"].h, v_p[T - 2][0])
            for c in range(C_CHUNKS):
                last = t.matmul(b["P3"][:, c * 128:(c + 1) * 128],
                                b["kactT"][:, c * 128:(c + 1) * 128],
                                b["ident"][:], start=True, stop=True)
            last.then_inc(S["t"].h, 1)
            if T + 1 < T_TILES:
                gproj(T + 1)
            t.wait_ge(S["v"].h, v_p[T][0])
            pso = b["P0"] if T % 2 == 0 else b["P1"]
            if T >= 2:
                t.wait_ge(S["v"].h, v_p[T - 2][1])
            for c in range(C_CHUNKS):
                for hh in range(2):
                    last = t.matmul(
                        pso[:, hh * 512:(hh + 1) * 512],
                        b["kact"][:, c * 128:(c + 1) * 128],
                        wslot(b, 1)[:, c, hh * 512:(hh + 1) * 512],
                        start=(c == 0), stop=(c == C_CHUNKS - 1))
            last.then_inc(S["t"].h, 1)
    p.emit("tensor", attn_tensor3)

    tk01 = [b["tkbf"][:, 0, :], b["tkbf"][:, 1, :]]

    def attn_pool3(g, wgb=v_gb):
        g.wait_ge(S["w"].h, w_wo)
        g.wait_ge(S["v"].h, wgb)
        for T in range(T_TILES):
            g.wait_ge(S["v"].h, v_d0[T])
            if T >= 1:
                g.wait_ge(S["t"].h, t_y3t[T - 1])
            g.tensor_mul(b["kactT"][:], tk01[T % 2],
                         gB).then_inc(S["g"].h, 1)
            g.wait_ge(S["g"].h, g_p[T][1])
            g.tensor_add(tk01[T % 2], b["kactT"][:],
                         bB).then_inc(S["g"].h, 1)
            g.wait_ge(S["g"].h, g_p[T][2])
            g.wait_ge(S["s"].h, s_gate[T])
            g.tensor_mul(b["kactT"][:], tk01[T % 2],
                         b["bfA"][:]).then_inc(S["g"].h, 1)
    p.emit("gpsimd", attn_pool3)

    def attn_vec3(v, tst=p.want_tail_stats):
        v.wait_ge(S["v"].h, v_rstd)

        def d0(T):
            yv = arena3[:, :, T * 128:(T + 1) * 128]
            if T >= 2:
                v.wait_ge(S["g"].h, g_p[T - 2][3])
            v.tensor_scalar(tk01[T % 2].rearrange("p (c j) -> p c j",
                                                  c=C_CHUNKS),
                            yv, b["negmean"][:, T:T + 1],
                            b["rstd"][:, T:T + 1], ALU.add,
                            ALU.mult).then_inc(S["v"].h, 1)

        d0(0); d0(1)
        for T in range(T_TILES):
            v.wait_ge(S["t"].h, t_y3t[T])
            if T >= 1:
                v.wait_ge(S["t"].h, t_wo[T - 1])
            v.tensor_copy(b["kact"][:], b["P3"][:]).then_inc(S["v"].h, 1)
            v.wait_ge(S["t"].h, t_wo[T])
            pso = b["P0"] if T % 2 == 0 else b["P1"]
            v.tensor_add(b["x"][:, T, :], b["x"][:, T, :], pso[:]
                         ).then_inc(S["v"].h, 1)
            if tst:
                v.wait_ge(S["v"].h, v_p[T][1])
                v.bn_stats(b["bn6a"][:, T, 0:6], b["x"][:, T, 0:512])
                v.bn_stats(b["bn6a"][:, T, 6:12], b["x"][:, T, 512:1024]
                           ).then_inc(S["v"].h, 1)
            if T + 2 < T_TILES:
                d0(T + 2)
    p.emit("vector", attn_vec3)
    if p.want_tail_stats:
        p.pending_stats = v_bn

    p.wslot_free = [t_q[-1], t_wo[-1], t_g[-1]]
    return dict(done_v=(v_bn[-1] if p.want_tail_stats else v_p[-1][1]))


def build_program(parts):
    nc = bass.Bass("TRN2", target_bir_lowering=False)
    dram = {}

    def din(name, shape, dtype=F32):
        dram[name] = nc.dram_tensor(name, shape, dtype,
                                    kind="ExternalInput").ap()
        return dram[name]

    xin = din("xin", [TOK, DM])
    xout = nc.dram_tensor("xout", [TOK, DM], F32, kind="ExternalOutput").ap()
    din("ident", [128, 128], BF16)
    din("onesbf", [128, 128], BF16)
    if any(s.startswith("l0ffn") for s in parts):
        din("muk", [128, 8]); din("mur", [128, 8])
        din("wr", [DM, DM]); din("wk", [DM, DM]); din("wv", [DM, DM])
    if "l0attn" in parts:
        for nm in ("wq0", "wk00", "wk01", "wv00", "wv01", "wo0"):
            din(nm, [DM, DM])
        din("lnbc", [128, 8])
        din("bpk0f", [128, DM]); din("bpk1f", [128, DM])
        din("lngc", [128, 8])
    if any(s.startswith("l1ffn") for s in parts):
        din("w2t", [16384, DM])
        din("rmt", [4, 64, 16, 16]); din("rupt", [4, 64, 16, 64])
        dram["meffraw"] = nc.dram_tensor("meffraw", [16 * 16 * 16 * 64],
                                         BF16, kind="Internal").ap()
        dram["meffscratch"] = nc.dram_tensor("meffscratch", [8 * 128 * 2048],
                                             BF16, kind="Internal").ap()
    if "l1attn" in parts:
        for nm in ("wq1", "wv1", "wg1", "wo1"):
            din(nm, [DM, DM])
        din("gbpack", [128, DM]); din("lamc", [128, 8])

    import contextlib
    es = contextlib.ExitStack()
    sb = lambda n, sh, dt: es.enter_context(nc.sbuf_tensor(n, sh, dt))
    psal = lambda n, sh: es.enter_context(nc.psum_tensor(n, sh, F32))
    b = {}
    b["x"] = sb("x", [128, T_TILES, DM], F32)
    b["xlnT"] = sb("xlnT", [128, C_CHUNKS, TOK], BF16)
    b["arena"] = sb("arena", [128, 16384], BF16)
    b["wbuf"] = sb("wbuf", [128, 3, 8192], BF16)
    b["ident"] = sb("identsb", [128, 128], BF16)
    b["xln_t"] = sb("xln_t", [128, 2, DM], BF16)
    b["bn6a"] = sb("bn6a", [128, T_TILES, 12], F32)
    b["stw"] = sb("stw", [128, T_TILES, 8], F32)
    for nm in ("m_s", "dm", "cv_s", "dm2"):
        b[nm] = sb(nm, [128, T_TILES], F32)
    for nm in ("negmean", "varpe", "sqv", "rstd"):
        b[nm] = sb(nm, [128, T_TILES], F32)
    for nm in ("muk", "mur", "ommk", "dmu"):
        b[nm] = sb(nm + "sb", [128, 8], F32)
    b["fA"] = sb("fA", [128, DM], F32)
    b["fB"] = sb("fB", [128, DM], F32)
    b["bfA"] = sb("bfA", [128, DM], BF16)
    b["onesbf"] = sb("onesbfsb", [128, 128], BF16)
    b["nmr"] = sb("nmr", [128, T_TILES], F32)
    b["relu_t"] = b["fA"][:]
    b["ytile"] = b["fB"][:]
    b["rbuf"] = b["bfA"][:]
    b["kact"] = b["xln_t"][:, 0, :]
    b["kactT"] = b["xln_t"][:, 1, :]
    b["tkbf"] = sb("tkbf", [128, 4, DM], BF16)
    if "wq0" in dram:
        b["ssq"] = sb("ssq", [128, T_TILES, 16], F32)
        b["ssq_s"] = sb("ssq_s", [128, T_TILES, 16], F32)
        b["rq"] = b["ssq"][:]
        b["ssk"] = sb("ssk", [128, 4, 16], F32)
        b["ssk_s"] = sb("ssk_s", [128, 4, 16], F32)
        b["rk"] = sb("rk", [128, 4, 16], F32)
        b["st0"] = b["fA"][:, 0:512]
        b["stateb"] = b["fB"][:, 0:512].bitcast(BF16)
        b["lnbc"] = sb("lnbcsb", [128, 8], BF16)
        b["halfc"] = sb("halfc", [128, 1], F32)
        b["lngc"] = sb("lngcsb", [128, 8], F32)
        b["bpk0"] = sb("bpk0", [128, DM], BF16)
        b["bpk1"] = sb("bpk1", [128, DM], BF16)
        b["rowtmp"] = b["bfA"][0:1, :]
        b["biaspB"] = b["tkbf"][:, 0, :]
        b["rowsumB"] = b["tkbf"][:, 1, :]
    if "wq1" in dram:
        if "bpk0" in b:
            b["gbp"] = b["bpk0"][:]
        else:
            b["gbp"] = sb("gbp", [128, DM], BF16)
        b["lamc"] = sb("lamcsb", [128, 8], F32)
        b["bn6b"] = b["fB"][:, 0:768].rearrange("p (t g s) -> p t g s",
                                                t=T_TILES, g=8)
        if "ssq" in b:
            b["sq16"] = b["ssq"][:]
        else:
            b["sq16"] = sb("sq16", [128, T_TILES, 16], F32)
    if any(s.startswith("l1ffn") for s in parts) and "bpk1" not in b:
        b["bpk1"] = sb("bpk1", [128, DM], BF16)
    b["P0"] = psal("P0", [128, 1024])
    b["P1"] = psal("P1", [128, 1024])
    b["P2"] = psal("P2", [128, 1024])
    b["P3"] = psal("P3", [128, 1024])

    sems = {}
    for name in ("x", "w", "t", "v", "s", "o", "pl", "g"):
        h = es.enter_context(nc.semaphore("sem_" + name))
        sems[name] = Sem(h, 16 if name in ("x", "w", "o", "pl") else 1)

    p = Prog(nc, b, sems, dram)
    p.wslot_free = [0, 0, 0]
    S = p.S

    n_setup = 3 + (2 if any(s.startswith("l0ffn") for s in parts) else 0)
    x_loaded = S["x"].bump(n_setup)
    def setup_sync(sy):
        sy.dma_start(out=b["x"][:],
                     in_=xin.rearrange("(T p) c -> p T c",
                                       p=128)).then_inc(S["x"].h, 16)
        sy.dma_start(out=b["ident"][:], in_=dram["ident"]).then_inc(S["x"].h, 16)
        sy.dma_start(out=b["onesbf"][:],
                     in_=dram["onesbf"]).then_inc(S["x"].h, 16)
        if "muk" in dram:
            sy.dma_start(out=b["muk"][:],
                         in_=dram["muk"]).then_inc(S["x"].h, 16)
            sy.dma_start(out=b["mur"][:], in_=dram["mur"]).then_inc(S["x"].h, 16)
    p.emit("sync", setup_sync)
    if "wq0" in dram:
        att_sm = S["w"].bump(3)
        def att_smalls(g, w0=att_sm - 48):
            if w0:
                g.wait_ge(S["w"].h, w0)
            g.dma_start(out=b["bpk0"][:], in_=dram["bpk0f"]).then_inc(S["w"].h, 16)
            g.wait_ge(S["w"].h, w0 + 16)
            g.dma_start(out=b["bpk1"][:], in_=dram["bpk1f"]).then_inc(S["w"].h, 16)
            g.wait_ge(S["w"].h, w0 + 32)
            g.dma_start(out=b["lngc"][:], in_=dram["lngc"]).then_inc(S["w"].h, 16)
        p.emit("gpsimd", att_smalls)
        for eng in ("tensor", "vector"):
            p.emit(eng, lambda e, w=att_sm: e.wait_ge(S["w"].h, w))

    if any(s.startswith("l0ffn") for s in parts):
        v_setup = S["v"].bump()
        def setup_consts(v, w=x_loaded):
            v.wait_ge(S["x"].h, w)
            v.tensor_scalar(b["ommk"][:], b["muk"][:], -1.0, 1.0, ALU.mult,
                            ALU.add)
            v.tensor_sub(b["dmu"][:], b["mur"][:],
                         b["muk"][:]).then_inc(S["v"].h, 1)
        p.emit("vector", setup_consts)

    for eng in ("tensor", "scalar", "vector"):
        p.emit(eng, lambda e, w=x_loaded: e.wait_ge(S["x"].h, w))

    last_v = 0
    l1prep = None
    if any(s.startswith("l1ffn") for s in parts):
        l1prep = emit_l1prep(p)
    last_g = 0
    for pi, part in enumerate(parts):
        p.want_tail_stats = (pi + 1 < len(parts))
        if part in ("l0ffn_a", "l0ffn_b"):
            r = emit_lnffn(p, last_v, last_g)
            last_v = r["done_v"]
            last_g = r.get("done_g", 0) or last_g
            continue
        if part == "l0attn":
            r = emit_l0attn(p, last_v, last_g)
            last_v = r["done_v"]
            last_g = r.get("done_g", 0) or last_g
            continue
        ln = emit_ln(p, last_v, last_g, mix_l0ffn=False)
        if part in ("l1ffn_a", "l1ffn_b"):
            r = emit_l1ffn(p, ln, l1prep)
        elif part == "l1attn":
            r = emit_l1attn(p, ln)
        else:
            raise NotImplementedError(part)
        last_v = r["done_v"]
        last_g = r.get("done_g", 0) or last_g

    o_done = S["o"].bump()
    def store(sy, w=last_v, wg=last_g, wo=o_done):
        sy.wait_ge(S["v"].h, w)
        if wg:
            sy.wait_ge(S["g"].h, wg)
        sy.dma_start(out=xout.rearrange("(T p) c -> p T c", p=128),
                     in_=b["x"][:]).then_inc(S["o"].h, 16)
        sy.wait_ge(S["o"].h, wo)
    p.emit("sync", store)

    with es, nc.Block() as block:
        @block.sync
        def _(e):
            for fn in p.ops["sync"]:
                fn(e)
        @block.gpsimd
        def _(e):
            for fn in p.ops["gpsimd"]:
                fn(e)
        @block.tensor
        def _(e):
            for fn in p.ops["tensor"]:
                fn(e)
        @block.vector
        def _(e):
            for fn in p.ops["vector"]:
                fn(e)
        @block.scalar
        def _(e):
            for fn in p.ops["scalar"]:
                fn(e)
    return nc


def host_inputs(inputs, core, parts):
    m = {}
    m["xin"] = np.ascontiguousarray(np.asarray(inputs["L0_attn_x"])[core])
    m["ident"] = np.eye(128, dtype=np.float32).astype(BF)
    m["onesbf"] = np.ones((128, 128), dtype=np.float32).astype(BF)
    if "l0attn" in parts:
        f32 = np.float32
        m["wq0"] = np.ascontiguousarray(np.asarray(inputs["L0_attn_Wq"], f32).T)
        m["wk00"] = np.ascontiguousarray(np.asarray(inputs["L0_attn_Wk"], f32)[0].T)
        m["wk01"] = np.ascontiguousarray(np.asarray(inputs["L0_attn_Wk"], f32)[1].T)
        m["wv00"] = np.ascontiguousarray(np.asarray(inputs["L0_attn_Wv"], f32)[0].T)
        m["wv01"] = np.ascontiguousarray(np.asarray(inputs["L0_attn_Wv"], f32)[1].T)
        m["wo0"] = np.ascontiguousarray(np.asarray(inputs["L0_attn_Wo"], f32).T)
        m["lnbc"] = np.ascontiguousarray(
            np.asarray(inputs["L0_attn_ln_b"], f32).reshape(8, 128).T)
        m["lngc"] = np.ascontiguousarray(
            np.asarray(inputs["L0_attn_ln_g"], f32).reshape(8, 128).T)
        bp0 = np.zeros((128, DM), f32)
        bp0[0] = np.asarray(inputs["L0_attn_bq"], f32)
        bp0[32] = np.asarray(inputs["L0_attn_bk"], f32)[0]
        bp0[64] = np.asarray(inputs["L0_attn_bk"], f32)[1]
        m["bpk0f"] = bp0
        bp1 = np.zeros((128, DM), f32)
        bp1[0] = np.asarray(inputs["L0_attn_bv"], f32)[0]
        bp1[32] = np.asarray(inputs["L0_attn_bv"], f32)[1]
        bp1[64] = np.asarray(inputs["L0_attn_bo"], f32)
        m["bpk1f"] = bp1
    f32 = np.float32
    if any(s.startswith("l1ffn") for s in parts):
        m["w2t"] = np.ascontiguousarray(
            np.asarray(inputs["L1_ffn_w_cproj"], f32).T)
        m["rmt"] = np.ascontiguousarray(
            np.asarray(inputs["L1_ffn_rail_maker"], f32).transpose(0, 1, 3, 2))
        m["rupt"] = np.ascontiguousarray(
            np.asarray(inputs["L1_ffn_rail_up_proj"],
                       f32).transpose(0, 1, 3, 2))
    if "l1attn" in parts:
        m["wq1"] = np.ascontiguousarray(np.asarray(inputs["L1_attn_Wq"], f32).T)
        m["wv1"] = np.ascontiguousarray(np.asarray(inputs["L1_attn_Wv"], f32).T)
        m["wg1"] = np.ascontiguousarray(np.asarray(inputs["L1_attn_Wg"], f32).T)
        m["wo1"] = np.ascontiguousarray(np.asarray(inputs["L1_attn_Wo"], f32).T)
        gb = np.zeros((128, DM), f32)
        gb[0] = np.asarray(inputs["L1_attn_ln_gamma"], f32)
        gb[32] = np.asarray(inputs["L1_attn_ln_beta"], f32)
        m["gbpack"] = gb
        m["lamc"] = np.ascontiguousarray(
            np.repeat(np.asarray(inputs["L1_attn_decay"], f32),
                      64).reshape(8, 128).T)
    if any(s.startswith("l0ffn") for s in parts):
        m["muk"] = np.ascontiguousarray(
            np.asarray(inputs["L0_ffn_mu_k"], np.float32).reshape(8, 128).T)
        m["mur"] = np.ascontiguousarray(
            np.asarray(inputs["L0_ffn_mu_r"], np.float32).reshape(8, 128).T)
        m["wr"] = np.ascontiguousarray(np.asarray(inputs["L0_ffn_W_r"],
                                                  np.float32).T)
        m["wk"] = np.ascontiguousarray(np.asarray(inputs["L0_ffn_W_k"],
                                                  np.float32).T)
        m["wv"] = np.ascontiguousarray(np.asarray(inputs["L0_ffn_W_v"],
                                                  np.float32).T)
    return m


def run_parts(inputs, parts, cores=8, xin_override=None):
    nc = build_program(parts)
    in_maps = []
    for core in range(cores):
        m = host_inputs(inputs, core, parts)
        if xin_override is not None:
            m["xin"] = np.ascontiguousarray(xin_override[core])
        in_maps.append(m)
    res = run_bass_kernel_spmd(nc, in_maps, core_ids=list(range(cores)))
    return np.stack([res.results[i]["xout"] for i in range(cores)])


ALL_PARTS = ["l0ffn_a", "l0attn", "l0ffn_b", "l1ffn_a", "l1attn", "l1ffn_b"]


def kernel(**inputs):
    return run_parts(inputs, ALL_PARTS).astype(np.float32)
